# version 13
# baseline (speedup 1.0000x reference)
import numpy as np
import ml_dtypes
import concourse.bass as bass
import concourse.mybir as mybir
from concourse.bass_utils import run_bass_kernel_spmd
from contextlib import ExitStack

F32 = mybir.dt.float32
BF16 = mybir.dt.bfloat16
I32 = mybir.dt.int32
U8 = mybir.dt.uint8
AF = mybir.ActivationFunctionType
OP = mybir.AluOpType
AX = mybir.AxisListType

S, D, H, HD, FF = 4096, 1024, 16, 64, 4096
NT, NC_, KD, NM = 32, 8, 8, 32
EPS = 1e-5
KB = 1024
ENG2SEM = {"t": "t", "v": "v", "s": "s"}


DMA_STREAMS = ("ld", "lds", "bl0", "bl1", "trA0", "trA1", "xnh0", "xnh1", "qkv0", "qkv1", "w",
               "ctxs", "ctxt", "ssp0", "ssp1", "dq", "trh", "trx", "trg", "gts0",
               "gts1", "y0", "y1")


class EM:

    def __init__(self):
        self.P = {k: [] for k in ("t", "v", "s", "sy", "g")}
        self.C = {k: 0 for k in ("t", "v", "s") + DMA_STREAMS}
        self.lastwait = {}

    def op(self, ek, fn, inc=False, sem=None):
        if inc or sem:
            sk = sem or ENG2SEM[ek]
            n = 16 if sk in DMA_STREAMS else 1
            self.C[sk] += n
            self.P[ek].append((fn, sk, n))
            return self.C[sk]
        self.P[ek].append((fn, None, 0))
        return None

    def wait(self, ek, sk, val=None):
        if sk in DMA_STREAMS:
            val = self.C[sk]
        if not val or val <= 0:
            return
        key = (ek, sk)
        if self.lastwait.get(key, -1) >= val:
            return
        self.lastwait[key] = val
        self.P[ek].append((("W", sk, val), None, 0))


def build(nc: bass.Bass, phases=("A", "B", "C"), dbg_spill=False, bopt=None):
    bopt = dict({"nqb": NC_, "nh": H, "do_av": True, "do_wo": True, "wo_stage": 4,
                 "s_nospill": False, "s_copy": False, "par": 0}, **(bopt or {}))
    ap = lambda name, shape, dt: nc.dram_tensor(name, shape, dt, kind="ExternalInput").ap()
    X = ap("x", [S, D], F32)
    Wq, Wk, Wv, Wo = (ap(n, [D, D], F32) for n in ("wq", "wk", "wv", "wo"))
    BO = ap("bo", [D], F32)
    REL = ap("rel", [17], F32)
    GAM, BET = ap("gam", [D], F32), ap("bet", [D], F32)
    Q1 = ap("q1", [D, FF], I32)
    SC1 = ap("sc1", [8, FF], F32)
    B1C = ap("b1c", [128, 32], F32)
    Q2 = ap("q2", [FF, D], I32)
    SC2 = ap("sc2", [32, D], F32)
    B2C = ap("b2c", [128, 8], F32)
    DIST = ap("dist", [128, 256], F32)
    ZM = ap("zm", [128, 128], BF16)

    Y = nc.dram_tensor("y", [S, D], F32, kind="ExternalOutput").ap()
    kscr = "ExternalOutput" if dbg_spill else "Internal"
    scr = lambda name, shape, dt: nc.dram_tensor(name, shape, dt, kind=kscr).ap()
    XN16D = scr("xn16d", [S, D], BF16)
    H16D = scr("h16d", [S, D], BF16)
    QTD = scr("qtd", [D, S], BF16)
    KTD = scr("ktd", [D, S], BF16)
    VAD = scr("vad", [S, H * 65], BF16)
    CTXD = scr("ctxd", [S, D], BF16)
    SD = scr("sd", [S, D], BF16)
    GTD = scr("gtd", [D, S], BF16)

    e = EM()

    def drain_guard(ek, n=2):
        mk = None
        for _ in range(n):
            if ek == "v":
                mk = e.op("v", lambda g: g.tensor_copy(junkc[:, 8:16], junkc[:, 0:8]),
                          inc=True)
            else:
                mk = e.op("s", lambda g: g.activation(out=junkc[:, 8:16],
                                                      in_=junkc[:, 0:8],
                                                      func=AF.Copy), inc=True)
        e.wait(ek, ENG2SEM[ek], mk)

    def parwait(ek):
        for sk in ("t", "v", "s") + DMA_STREAMS:
            if ENG2SEM.get(ek) != sk:
                e.wait(ek, sk, e.C[sk])
    with ExitStack() as es:
        E_ = es.enter_context
        AR = E_(nc.sbuf_tensor("arena", [128, 204 * KB], U8)).ap()

        def carve(off_kb, shape, dt):
            nb = int(np.prod(shape[1:])) * {F32: 4, BF16: 2, I32: 4}[dt]
            off = int(off_kb * KB)
            assert off % 64 == 0 and off + nb <= 204 * KB, (off_kb, shape)
            v = AR[:, off:off + nb].bitcast(dt)
            if len(shape) > 2:
                dims = {f"d{i}": shape[i] for i in range(1, len(shape))}
                pat = ("p (" + " ".join(dims) + ") -> p " + " ".join(dims))
                v = v.rearrange(pat, **{k: dims[k] for k in list(dims)[:-1]})
            return v

        PS = E_(nc.psum_tensor("ps", [128, 4096], F32)).ap() if False else \
            E_(nc.psum_tensor("ps", [128, 4096], F32))
        P012 = PS[:, 0:1536]
        P345 = PS[:, 1536:3072]
        P6 = PS[:, 3072:3584]
        P7 = PS[:, 3584:4096]
        sem = {k: E_(nc.semaphore(f"sem_{k}")) for k in ("t", "v", "s") + DMA_STREAMS}
        blk = E_(nc.Block())

        xnT = carve(0, [128, KD, S], BF16)
        wq_b = carve(64, [128, KD, D], BF16)
        wk_b = carve(80, [128, KD, D], BF16)
        wv_b = carve(96, [128, KD, D], BF16)
        x_ring = carve(112, [128, 4, D], F32)
        sq_scr = carve(128, [128, D], F32)
        ht_scr = carve(132, [128, D], F32)
        xn_st = carve(136, [128, 2, D], BF16)
        h_st = carve(140, [128, 2, D], BF16)
        q_st = carve(144, [128, 2, 512], BF16)
        k_st = carve(146, [128, 2, 512], BF16)
        v_st = carve(148, [128, 2, H, 65], BF16)
        gam_b = carve(153, [128, D], F32)
        bet_b = carve(157, [128, D], F32)
        colsA = carve(161, [128, 64], F32)
        bo_b = carve(187, [128, D], F32)
        B_dc = carve(191, [128, 256], F32)
        zm_sb = carve(192, [128, 128], BF16)
        rel_bc = carve(192.25, [128, 17], F32)
        r16c = carve(192.375, [128, 1], F32)
        ri16c = carve(192.4375, [128, 1], F32)
        a16c = carve(192.5, [128, 1], F32)
        epsc = carve(192.5625, [128, 1], F32)
        tmpB = carve(193, [128, 256], F32)
        tmpB2 = carve(195, [128, 256], F32)
        rlc = carve(196, [128, 8], F32)
        kT = carve(0, [128, KD, S], BF16)
        v_sb = carve(64, [128, NT, H * 65], BF16)
        qT = carve(129, [128, 2, KD, 512], BF16)
        E_sb = carve(145, [128, 2, 2048], BF16)
        wo_b = carve(153, [128, KD, D], BF16)
        ctx_sb = carve(169, [128, 4, D], BF16)
        ctxT = carve(177, [128, KD, 512], BF16)
        s_st = carve(185, [128, 2, 512], BF16)
        w1_b = carve(0, [128, 8, FF], BF16)
        w2_b = carve(64, [128, 32, D], BF16)
        yT = carve(128, [128, NM, 512], BF16)
        q1q = carve(129, [128, 2, 1024], I32)
        sc1row_q = carve(137, [128, 2, 1024], F32)
        ones1 = carve(194, [128, 128], F32)
        hT_c = carve(160, [128, KD, 512], BF16)
        xnT_c = carve(168, [128, KD, 512], BF16)
        x_c = carve(176, [128, 2, D], F32)
        s_c = carve(184, [128, 2, D], BF16)
        gt_st = carve(188, [128, 2, 512], BF16)
        g_st = carve(190, [128, 2, D], BF16)
        y_st = carve(194, [128, 2, D], F32)
        b1col = carve(202, [128, 32], F32)
        junkc = carve(203, [128, 16], F32)
        hb2col = carve(202.25, [128, 8], F32)

        def dma(ek, st, out, in_, **kw):
            return e.op(ek, lambda g, out=out, in_=in_, kw=kw: g.dma_start(out=out, in_=in_, **kw), sem=st)

        def vop(fn, inc=False):
            return e.op("v", fn, inc=inc)

        def sop(fn, inc=False):
            return e.op("s", fn, inc=inc)

        m_wcast = None
        for (w_sb, Wsrc) in ((wq_b, Wq), (wk_b, Wk), (wv_b, Wv)):
            Wr = Wsrc.rearrange("(k p) n -> k p n", p=128)
            for k in range(KD):
                m_wcast = dma("g", "w", w_sb[:, k, :], Wr[k])
        m_gb = dma("sy", "ld", gam_b[:], GAM.unsqueeze(0).to_broadcast((128, D)))
        m_bb = dma("sy", "ld", bet_b[:], BET.unsqueeze(0).to_broadcast((128, D)))
        dma("sy", "ld", bo_b[:], BO.unsqueeze(0).to_broadcast((128, D)))
        dma("sy", "ld", rel_bc[:], REL.unsqueeze(0).to_broadcast((128, 17)))
        m_r16 = dma("sy", "ld", r16c[:], REL[16:17].unsqueeze(0).to_broadcast((128, 1)))
        m_dist = dma("sy", "ld", tmpB2[:], DIST)
        m_zm = dma("sy", "ld", zm_sb[:], ZM)
        m_b1c = dma("sy", "ld", b1col[:], B1C)
        m_b2craw = dma("sy", "ld", hb2col[:], B2C)

        e.wait("v", "ld")
        vop(lambda g: g.memset(epsc[:], EPS), inc=True)
        vop(lambda g: g.memset(v_st[:, 0, :, 64:65], 1.0), inc=True)
        vop(lambda g: g.memset(v_st[:, 1, :, 64:65], 1.0), inc=True)
        vop(lambda g: g.memset(ones1[0:1, :], 1.0), inc=True)
        vop(lambda g: g.reciprocal(ri16c[:], r16c[:]), inc=True)
        mk0 = vop(lambda g: g.tensor_scalar(out=a16c[:], in0=r16c[:], scalar1=0.125,
                                            scalar2=None, op0=OP.mult), inc=True)
        e.wait("v", "v", mk0)
        mk0 = vop(lambda g: g.tensor_scalar(out=rel_bc[:], in0=rel_bc[:],
                                            scalar1=ri16c[:], scalar2=None,
                                            op0=OP.mult), inc=True)
        e.wait("v", "v", mk0)
        e.wait("v", "ld")
        for r in range(17):
            dst = B_dc if r == 0 else tmpB
            mk0 = vop(lambda g, r=r, dst=dst: g.tensor_scalar(
                out=dst[:], in0=tmpB2[:], scalar1=float(r), scalar2=rel_bc[:, r:r + 1],
                op0=OP.is_equal, op1=OP.mult), inc=True)
            e.wait("v", "v", mk0)
            if r > 0:
                mk0 = vop(lambda g: g.tensor_tensor(out=B_dc[:], in0=B_dc[:],
                                                    in1=tmpB[:], op=OP.add), inc=True)
                e.wait("v", "v", mk0)
        e.wait("v", "ld")
        m_setup = vop(lambda g: g.tensor_scalar(out=hb2col[:], in0=hb2col[:],
                                                scalar1=0.5, scalar2=None, op0=OP.mult),
                      inc=True)
        v_setup_end = m_setup
        s_prev_chunk = [0]

        Xr = X.rearrange("(i p) d -> i p d", p=128)
        XN16r = XN16D.rearrange("(i p) d -> i p d", p=128)
        H16r = H16D.rearrange("(i p) d -> i p d", p=128)
        QTr = QTD.rearrange("(m p) q -> m p q", p=128)
        KTr = KTD.rearrange("(m p) q -> m p q", p=128)
        VAr = VAD.rearrange("(i p) c -> i p c", p=128)

        m_xload = [0] * NT
        m_a1p1 = [0] * NT
        m_sq = [0] * NT
        m_hv = [0] * NT
        m_xnv = [0] * NT
        m_xnsp = [0] * NT
        m_hsp = [0] * NT
        m_tr = [0] * NC_
        qk_ev = {}
        qk_sp = {}
        v_ev = {}
        v_sp = {}
        last_bank_ev = {}
        C_ = lambda i, j: colsA[:, (i % 4) * 16 + j:(i % 4) * 16 + j + 1]

        def emit_A1(n):
            tiles = list(range(4 * n, 4 * n + 4))
            for i in tiles:
                if i >= 4:
                    e.wait("s", "v", m_hv[i - 4])
                m_xload[i] = dma("s", "lds", x_ring[:, i % 4, :], Xr[i])
            e.wait("v", "lds")
            e.wait("v", "v", v_setup_end)
            if n > 0:
                e.wait("v", "s", s_prev_chunk[0])
            for i in tiles:
                xt = x_ring[:, i % 4, :]
                e.wait("v", "ld")
                vop(lambda g, xt=xt: g.tensor_tensor(out=sq_scr[:], in0=xt, in1=xt,
                                                     op=OP.mult))
                vop(lambda g, i=i: g.tensor_reduce(out=C_(i, 0), in_=sq_scr[:],
                                                   axis=AX.X, op=OP.add), inc=True)
                mk = vop(lambda g, xt=xt, i=i: g.tensor_reduce(
                    out=C_(i, 1), in_=xt, axis=AX.X, op=OP.add), inc=True)
            e.wait("v", "v", mk)
            for i in tiles:
                vop(lambda g, i=i: g.tensor_scalar(out=C_(i, 2), in0=C_(i, 1),
                                                   scalar1=1.0 / D, scalar2=None,
                                                   op0=OP.mult), inc=True)
                mk = vop(lambda g, i=i: g.tensor_scalar(
                    out=C_(i, 4), in0=C_(i, 0), scalar1=1.0 / D, scalar2=epsc[:],
                    op0=OP.mult, op1=OP.add), inc=True)
            e.wait("v", "v", mk)
            for i in tiles:
                mk = vop(lambda g, i=i: g.tensor_tensor(out=C_(i, 3), in0=C_(i, 2),
                                                        in1=C_(i, 2), op=OP.mult),
                         inc=True)
            e.wait("v", "v", mk)
            for i in tiles:
                mk = vop(lambda g, i=i: g.tensor_tensor(out=C_(i, 5), in0=C_(i, 4),
                                                        in1=C_(i, 3), op=OP.subtract),
                         inc=True)
            e.wait("s", "v", mk)
            for i in tiles:
                mks = sop(lambda g, i=i: g.activation(out=C_(i, 7), in_=C_(i, 5),
                                                      func=AF.Sqrt), inc=True)
            e.wait("v", "s", mks)
            for i in tiles:
                vop(lambda g, i=i: g.reciprocal(C_(i, 8), C_(i, 7)), inc=True)
                mk = vop(lambda g, i=i: g.reciprocal(C_(i, 6), C_(i, 5)), inc=True)
            e.wait("s", "v", mk)
            for i in tiles:
                mks = sop(lambda g, i=i: g.activation(out=C_(i, 9), in_=C_(i, 6),
                                                      func=AF.Copy, scale=-EPS,
                                                      bias=1.0 + EPS), inc=True)
            e.wait("s", "s", mks)
            for i in tiles:
                mks = sop(lambda g, i=i: g.activation(out=C_(i, 10), in_=C_(i, 9),
                                                      func=AF.Sqrt), inc=True)
            s_prev_chunk[0] = mks
            e.wait("v", "s", mks)
            for i in tiles:
                mk = vop(lambda g, i=i: g.reciprocal(C_(i, 11), C_(i, 10)), inc=True)
            e.wait("v", "v", mk)
            for i in tiles:
                mk = vop(lambda g, i=i: g.tensor_tensor(out=C_(i, 13), in0=C_(i, 8),
                                                        in1=C_(i, 11), op=OP.mult),
                         inc=True)
            e.wait("v", "v", mk)
            for i in tiles:
                vop(lambda g, i=i: g.tensor_tensor(out=C_(i, 12), in0=C_(i, 2),
                                                   in1=C_(i, 8), op=OP.mult), inc=True)
                mk = vop(lambda g, i=i: g.tensor_tensor(out=C_(i, 14), in0=C_(i, 2),
                                                        in1=C_(i, 13), op=OP.mult),
                         inc=True)
            e.wait("v", "v", mk)
            for i in tiles:
                xt = x_ring[:, i % 4, :]
                sl = i % 2
                if i >= 2:
                    e.wait("v", "xnh%d" % (((i - 2) // 4) % 2))
                m_xnv[i] = vop(lambda g, xt=xt, i=i, sl=sl: g.tensor_scalar(
                    out=xn_st[:, sl, :], in0=xt, scalar1=C_(i, 8), scalar2=C_(i, 12),
                    op0=OP.mult, op1=OP.subtract), inc=True)
                vop(lambda g, xt=xt, i=i: g.tensor_scalar(
                    out=ht_scr[:], in0=xt, scalar1=C_(i, 13), scalar2=C_(i, 14),
                    op0=OP.mult, op1=OP.subtract))
                vop(lambda g: g.tensor_tensor(out=ht_scr[:], in0=ht_scr[:],
                                              in1=gam_b[:], op=OP.mult))
                m_hv[i] = vop(lambda g, sl=sl: g.tensor_tensor(
                    out=h_st[:, sl, :], in0=ht_scr[:], in1=bet_b[:], op=OP.add),
                    inc=True)
                e.wait("g", "v", m_xnv[i])
                m_xnsp[i] = dma("g", "xnh%d" % ((i // 4) % 2), XN16r[i], xn_st[:, sl, :])
                e.wait("g", "v", m_hv[i])
                m_hsp[i] = dma("g", "xnh%d" % ((i // 4) % 2), H16r[i], h_st[:, sl, :])
            e.wait("sy", "xnh%d" % (n % 2))
            for k in range(KD):
                m_tr[n] = dma("sy", "trA%d" % (n % 2), xnT[:, k, 512 * n:512 * (n + 1)],
                              XN16D[512 * n:512 * (n + 1), 128 * k:128 * (k + 1)],
                              transpose=True)

        def emit_A2(n):
            tiles = list(range(4 * n, 4 * n + 4))
            e.wait("t", "trA%d" % (n % 2))
            if n == 0:
                e.wait("t", "w")
                e.wait("t", "v", m_setup)
                e.wait("s", "v", m_setup)
            nwin = slice(512 * n, 512 * (n + 1))
            for (nm, w_sb, stg, dstD) in (("q", wq_b, q_st, QTr), ("k", wk_b, k_st, KTr)):
                for m in range(KD):
                    gidx = n * KD + m
                    ps = (P6 if m % 2 == 0 else P7)[:, 0:512]
                    e.wait("t", "s", last_bank_ev.get(m % 2, 0))
                    for k in range(KD):
                        e.op("t", lambda g, ps=ps, w=w_sb[:, k, 128 * m:128 * (m + 1)],
                             r=xnT[:, k, nwin], k=k: g.matmul(
                                 ps, w, r, start=(k == 0), stop=(k == KD - 1),
                                 skip_group_check=True),
                             inc=(k == KD - 1))
                    tmark = e.C["t"]
                    sl = m % 2
                    e.wait("s", "t", tmark)
                    e.wait("s", "qkv%d" % (gidx % 2))
                    if nm == "q":
                        mev = sop(lambda g, ps=ps, sl=sl: g.activation(
                            out=q_st[:, sl, :], in_=ps, func=AF.Copy), inc=True)
                        e.wait("g", "s", mev)
                        qk_sp[(nm, gidx)] = dma("g", "qkv%d" % (gidx % 2), dstD[m][:, nwin], q_st[:, sl, :])
                    else:
                        mev = sop(lambda g, ps=ps, sl=sl: g.activation(
                            out=k_st[:, sl, :], in_=ps, func=AF.Copy,
                            scale=a16c[:]), inc=True)
                        e.wait("g", "s", mev)
                        qk_sp[(nm, gidx)] = dma("g", "qkv%d" % (gidx % 2), dstD[m][:, nwin], k_st[:, sl, :])
                    qk_ev[(nm, gidx)] = mev
                    last_bank_ev[m % 2] = mev
            for i in tiles:
                pv = (P012 if i % 2 == 0 else P345)[:, 0:1024]
                e.wait("t", "s", v_ev.get(i - 2, 0))
                for k in range(KD):
                    for hh in range(2):
                        e.op("t", lambda g, pv=pv, hh=hh,
                             l=xnT[:, k, 128 * i:128 * (i + 1)],
                             r=wv_b[:, k, 512 * hh:512 * (hh + 1)], k=k: g.matmul(
                                 pv[:, 512 * hh:512 * (hh + 1)], l, r,
                                 start=(k == 0), stop=(k == KD - 1),
                                 skip_group_check=True),
                             inc=(k == KD - 1 and hh == 1))
                tmark = e.C["t"]
                sl = i % 2
                e.wait("s", "t", tmark)
                e.wait("s", "qkv%d" % (i % 2))
                v_ev[i] = sop(lambda g, pv=pv, sl=sl: g.activation(
                    out=v_st[:, sl, :, 0:64],
                    in_=pv.rearrange("p (h e) -> p h e", h=H), func=AF.Copy),
                    inc=True)
                e.wait("g", "s", v_ev[i])
                e.wait("g", "v", m_setup)
                v_sp[i] = dma("g", "qkv%d" % (i % 2), VAr[i], v_st[:, sl].rearrange("p h e -> p (h e)"))

        for n in range(NC_ + 2):
            if n >= 2:
                emit_A2(n - 2)
            if n < NC_:
                emit_A1(n)

        A_end_t = e.C["t"]
        A_end_v = e.C["v"]

        if "B" not in phases:
            _emit_block(nc, e, blk, sem, None)
            return nc

        e.wait("g", "v", A_end_v)
        wo_cast = None
        WoR = Wo.rearrange("(k p) n -> k p n", p=128)
        for k in range(KD):
            wo_cast = dma("g", "w", wo_b[:, k, :], WoR[k])

        e.wait("t", "v", A_end_v)
        e.wait("v", "ld")
        KTD_r = KTD.rearrange("(t p) q -> p t q", p=128)
        QTD_r = QTD.rearrange("(t p) q -> p t q", p=128)
        VAD_r = VAD.rearrange("(t p) c -> p t c", p=128)
        CTXr = CTXD.rearrange("(i p) d -> i p d", p=128)
        SDr = SD.rearrange("(i p) d -> i p d", p=128)

        exp_half = {}
        av_half = {}
        B_marks = {"scores": 0, "av": 0}
        last_scores_qb = [0] * NC_
        ctx_ev = {}
        ctx_sp = [0] * NC_
        ctxT_tr = [0] * NC_
        wo_end = [0] * NC_
        s_ev = {}
        s_sp = {}
        ci_glob = [0]

        def emit_wo_group(qbw, gidx):
            qs, half = gidx // 2, gidx % 2
            pso = (P6 if gidx % 2 == 0 else P7)[:, 0:512]
            prev_sev = s_ev.get((qbw, gidx - 2), s_ev.get((qbw - 1, gidx + 6), 0))
            e.wait("t", "v", prev_sev)
            e.wait("t", "ctxt")
            if qbw == 0 and gidx == 0:
                e.wait("t", "w")
            for k in range(KD):
                e.op("t", lambda g, pso=pso,
                     l=ctxT[:, k, 128 * qs:128 * (qs + 1)],
                     r=wo_b[:, k, 512 * half:512 * (half + 1)], k=k: g.matmul(
                         pso, l, r, start=(k == 0),
                         stop=(k == KD - 1), skip_group_check=True),
                     inc=(k == KD - 1))
            t_o = e.C["t"]
            sl = gidx % 2
            e.wait("v", "t", t_o)
            e.wait("v", "ssp%d" % (gidx % 2))
            s_ev[(qbw, gidx)] = vop(lambda g, sl=sl, half=half, pso=pso:
                                    g.tensor_tensor(
                                        out=s_st[:, sl, :], in0=pso,
                                        in1=bo_b[:, 512 * half:512 * (half + 1)],
                                        op=OP.add), inc=True)
            e.wait("g", "v", s_ev[(qbw, gidx)])
            s_sp[(qbw, gidx)] = dma(
                "g", "ssp%d" % (gidx % 2),
                SDr[4 * qbw + qs][:, 512 * half:512 * (half + 1)],
                s_st[:, sl, :])
            if gidx == 7:
                wo_end[qbw] = e.C["t"]

        def emit_b_loads(qb):
            e.wait("sy", "t", A_end_t)
            e.wait("sy", "xnh0"); e.wait("sy", "xnh1")
            e.wait("sy", "qkv0"); e.wait("sy", "qkv1")
            st = "bl%d" % (qb % 2)
            dma("sy", st, kT[:, :, 512 * qb:512 * (qb + 1)],
                KTD_r[:, :, 512 * qb:512 * (qb + 1)])
            dma("sy", st, v_sb[:, 4 * qb:4 * qb + 4, :],
                VAD_r[:, 4 * qb:4 * qb + 4, :])
            if qb >= 2:
                e.wait("sy", "t", last_scores_qb[qb - 2])
            dma("sy", st, qT[:, qb % 2], QTD_r[:, :, 512 * qb:512 * (qb + 1)])

        emit_b_loads(0)
        for qb in range(bopt["nqb"]):
            J = 4 * qb + 4

            jl = list(range(J))

            def emit_scores_half(u, j0, hp, jj):
                kt_tile = hp
                ebuf = E_sb[:, u % 2]
                j = j0 + jj
                sd_ = j - 4 * qb
                if jj == 0:
                    e.wait("t", "bl%d" % (qb % 2))
                e.wait("t", "s", exp_half.get((u - 1, jj), 0))
                for par in range(2):
                    poff = 64 * par
                    def smm(g, j=j, sd_=sd_, jj=jj, par=par, kt_tile=kt_tile,
                            poff=poff, qb2=qb % 2):
                        l_ap = kT[poff:poff + 64, kt_tile, 128 * j:128 * (j + 1)]
                        lo = 1024 * jj + 512 * par
                        if sd_ < 0:
                            r_ap = qT[poff:poff + 64, qb2, kt_tile, 0:512]
                            o_ap = PS[:, lo:lo + 512]
                        else:
                            r_ap = qT[poff:poff + 64, qb2, kt_tile, 128 * sd_:512]
                            o_ap = PS[:, lo + 128 * sd_:lo + 512]
                        return g.matmul(o_ap, l_ap, r_ap, start=True, stop=True,
                                        skip_group_check=True)
                    e.op("t", smm, inc=(par == 1))
                t_sc = e.C["t"]
                B_marks["scores"] = t_sc
                v_band = None
                breg = None
                if sd_ == -1:
                    breg = (0, 128, 128)
                elif 0 <= sd_ <= 3:
                    breg = (128 * sd_, min(256, 512 - 128 * sd_), 0)
                if breg is not None:
                    o0, w, boff = breg
                    e.wait("v", "t", t_sc)
                    for par in range(2):
                        lo = 1024 * jj + 512 * par + o0
                        v_band = vop(lambda g, lo=lo, w=w, boff=boff:
                                     g.tensor_tensor(out=PS[:, lo:lo + w],
                                                     in0=PS[:, lo:lo + w],
                                                     in1=B_dc[:, boff:boff + w],
                                                     op=OP.mult),
                                     inc=True)
                e.wait("s", "t", t_sc)
                if v_band:
                    e.wait("s", "v", v_band)
                e.wait("s", "t", av_half.get((u - 2, jj), 0))
                exp_half[(u, jj)] = sop(lambda g, ebuf=ebuf, jj=jj:
                                        g.activation(
                                            out=ebuf[0:128, 1024 * jj:1024 * jj + 1024],
                                            in_=PS[:, 1024 * jj:1024 * jj + 1024],
                                            func=AF.Exp), inc=True)
                v_z = None
                if 0 <= sd_ <= 3:
                    e.wait("v", "s", exp_half[(u, jj)])
                    for par in range(2):
                        col = 1024 * jj + 512 * par + 128 * sd_
                        v_z = vop(lambda g, ebuf=ebuf, col=col: g.tensor_tensor(
                            out=ebuf[0:128, col:col + 128],
                            in0=ebuf[0:128, col:col + 128], in1=zm_sb[:],
                            op=OP.mult), inc=True)
                zv_mark[(u, jj)] = v_z

            def emit_av_half(u, j0, hp, jj, first):
                ebuf = E_sb[:, u % 2]
                e.wait("t", "s", exp_half[(u, jj)])
                if zv_mark.get((u, jj)):
                    e.wait("t", "v", zv_mark[(u, jj)])
                if first:
                    prev = ctx_ev.get((qb, hp - 1), 0) if hp > 0 else \
                        (ctx_ev.get((qb - 1, H // 2 - 1), 0) if qb > 0 else 0)
                    e.wait("t", "v", prev)
                j = j0 + jj
                sd_ = j - 4 * qb
                for par in range(2):
                    h = 2 * hp + par
                    cbase = 2048 + 512 * par
                    for qs in range(max(0, sd_), 4):
                        def av(g, jj=jj, j=j, qs=qs, par=par, h=h, cbase=cbase,
                               ebuf=ebuf, J=J):
                            return g.matmul(
                                PS[:, cbase + 65 * qs:cbase + 65 * qs + 65],
                                ebuf[0:128, 1024 * jj + 512 * par + 128 * qs:
                                     1024 * jj + 512 * par + 128 * qs + 128],
                                v_sb[:, j, 65 * h:65 * (h + 1)],
                                start=(j == 0 and qs == 0),
                                stop=(j == J - 1 and qs == 3),
                                skip_group_check=True)
                        e.op("t", av, inc=(par == 1 and qs == 3))
                av_half[(u, jj)] = e.C["t"]
                B_marks["av"] = av_half[(u, jj)]

            zv_mark = {}
            for hp in range(H // 2):
                if hp == 1 and qb + 1 < bopt["nqb"]:
                    emit_b_loads(qb + 1)
                pend = None
                first = True
                for idx, j0 in enumerate(range(0, J, 2)):
                    u = ci_glob[0]
                    ci_glob[0] += 1
                    emit_scores_half(u, j0, hp, 0)
                    if pend is not None:
                        emit_av_half(pend[0], pend[1], hp, 0, first)
                    emit_scores_half(u, j0, hp, 1)
                    if pend is not None:
                        emit_av_half(pend[0], pend[1], hp, 1, False)
                        first = False
                    pend = (u, j0)
                    if idx == 0 and hp >= 1 and qb > 0 and bopt["do_wo"] \
                            and bopt["nh"] == H:
                        emit_wo_group(qb - 1, hp)
                emit_av_half(pend[0], pend[1], hp, 0, first)
                emit_av_half(pend[0], pend[1], hp, 1, False)
                if hp == 0 and qb > 0 and bopt["do_wo"] and bopt["nh"] == H:
                    emit_wo_group(qb - 1, hp)
                t_av = av_half[(pend[0], 1)]
                e.wait("v", "t", t_av)
                if hp == 0 and qb > 0:
                    e.wait("v", "ctxs")
                mkr = None
                for par in range(2):
                    cbase = 2048 + 512 * par
                    for qs in range(4):
                        mkr = vop(lambda g, qs=qs, par=par, cbase=cbase: g.reciprocal(
                            rlc[:, 4 * par + qs:4 * par + qs + 1],
                            PS[:, cbase + 65 * qs + 64:cbase + 65 * qs + 65]),
                            inc=True)
                e.wait("v", "v", mkr)
                for par in range(2):
                    h = 2 * hp + par
                    cbase = 2048 + 512 * par
                    for qs in range(4):
                        ctx_ev[(qb, hp)] = vop(lambda g, qs=qs, h=h, par=par,
                                               cbase=cbase: g.tensor_scalar(
                            out=ctx_sb[:, qs, 64 * h:64 * (h + 1)],
                            in0=PS[:, cbase + 65 * qs:cbase + 65 * qs + 64],
                            scalar1=rlc[:, 4 * par + qs:4 * par + qs + 1],
                            scalar2=None, op0=OP.mult),
                            inc=(par == 1 and qs == 3))
            last_scores_qb[qb] = e.C["t"]
            if not (bopt["do_av"] and bopt["do_wo"] and bopt["nh"] == H):
                continue
            e.wait("g", "v", ctx_ev[(qb, H // 2 - 1)])
            CTXr = CTXD.rearrange("(i p) d -> i p d", p=128)
            for qs in range(4):
                ctx_sp[qb] = dma("g", "ctxs", CTXr[4 * qb + qs], ctx_sb[:, qs, :])
            e.wait("sy", "ctxs")
            if qb > 0:
                e.wait("sy", "t", wo_end[qb - 1])
            for k in range(KD):
                ctxT_tr[qb] = dma("sy", "ctxt", ctxT[:, k, :],
                                  CTXD[512 * qb:512 * (qb + 1), 128 * k:128 * (k + 1)],
                                  transpose=True)
        if bopt["do_av"] and bopt["do_wo"] and bopt["nh"] == H and bopt["nqb"] == NC_:
            for gidx in range(8):
                emit_wo_group(NC_ - 1, gidx)

        B_end_t = e.C["t"]
        B_end_v = e.C["v"]
        B_end_s = e.C["s"]

        if "C" not in phases:
            _emit_block(nc, e, blk, sem, None)
            return nc

        Q1r = Q1.rearrange("(g p) f -> g p f", p=128)
        Q2r = Q2.rearrange("(g p) d -> g p d", p=128)
        B_scores_final = B_marks["scores"]
        B_last_av = B_marks["av"]
        e.wait("sy", "t", B_scores_final)
        e.wait("t", "s", B_end_s)
        all_stt = []
        bc_idx = 0

        def dq_unit(w_dst, q_src_ap, sc_row_slice, w2=False):
            psb = PS[:, 512 * (bc_idx % 2):512 * (bc_idx % 2) + 512]
            if bc_idx >= 2:
                e.wait("t", "v", all_stt[bc_idx - 2])
            e.op("t", lambda g, psb=psb, r=sc_row_slice: g.matmul(
                psb, ones1[0:1, :], r, start=True, stop=True,
                skip_group_check=True), inc=True)
            tbc = e.C["t"]
            e.wait("v", "t", tbc)
            all_stt.append(vop(lambda g, w_dst=w_dst, q=q_src_ap, psb=psb:
                               g.scalar_tensor_tensor(
                                   out=w_dst, in0=q, scalar=-128.0, in1=psb,
                                   op0=OP.add, op1=OP.mult), inc=True))

        for gix in range(8):
            for qt in range(4):
                it = gix * 4 + qt
                sl = it % 2
                if it >= 2:
                    e.wait("sy", "v", all_stt[2 * (it - 2) + 1])
                mq = dma("sy", "dq", q1q[:, sl, :], Q1r[gix][:, 1024 * qt:1024 * (qt + 1)])
                msc = dma("sy", "dq", sc1row_q[0:1, sl, :],
                          SC1[gix:gix + 1, 1024 * qt:1024 * (qt + 1)])
                e.wait("t", "dq")
                e.wait("v", "dq")
                for c in range(2):
                    dq_unit(
                        w1_b[:, gix, 1024 * qt + 512 * c:1024 * qt + 512 * (c + 1)],
                        q1q[:, sl, 512 * c:512 * (c + 1)],
                        sc1row_q[0:1, sl, 512 * c:512 * (c + 1)])
                    bc_idx += 1
        w1_done = all_stt[-1]
        e.wait("v", "t", B_last_av)
        e.wait("sy", "v", w1_done)
        n_w1 = len(all_stt)
        for g2 in range(32):
            sl = g2 % 2
            if g2 >= 2:
                e.wait("sy", "v", all_stt[n_w1 + 2 * (g2 - 2) + 1])
            mq = dma("sy", "dq", q1q[:, sl, :], Q2r[g2])
            msc = dma("sy", "dq", sc1row_q[0:1, sl, :], SC2[g2:g2 + 1, :])
            e.wait("t", "dq")
            e.wait("v", "dq")
            for c in range(2):
                dq_unit(w2_b[:, g2, 512 * c:512 * (c + 1)],
                        q1q[:, sl, 512 * c:512 * (c + 1)],
                        sc1row_q[0:1, sl, 512 * c:512 * (c + 1)], w2=True)
                bc_idx += 1
        w2_done = all_stt[-1]

        fc1_end = [0] * NC_
        fc2_end = [0] * NC_
        silu_mark = {}
        z_v = {}
        z_s = {}
        gt_sp = {}
        g_tr = [0] * NC_
        comb_v = {}
        y_dma = {}
        hT_ready = [0] * NC_
        xnT_ready = [0] * NC_
        sx_ready = {}

        def emit_hT_loads(n):
            if n == 0:
                e.wait("sy", "v", w1_done)
                e.wait("sy", "ctxs"); e.wait("sy", "ssp0"); e.wait("sy", "ssp1")
                e.wait("s", "ld")
            else:
                e.wait("sy", "t", fc1_end[n - 1])
            for k in range(KD):
                hT_ready[n] = dma("sy", "trh", hT_c[:, k, :],
                                  H16D[512 * n:512 * (n + 1), 128 * k:128 * (k + 1)],
                                  transpose=True)

        def emit_xnT_loads(n):
            if n > 0:
                e.wait("sy", "v", z_v.get((n - 1, 7), 0))
            for k in range(KD):
                xnT_ready[n] = dma("sy", "trx", xnT_c[:, k, :],
                                   XN16D[512 * n:512 * (n + 1), 128 * k:128 * (k + 1)],
                                   transpose=True)
        emit_hT_loads(0)
        emit_xnT_loads(0)
        for n in range(NC_):
            tiles = list(range(4 * n, 4 * n + 4))
            e.wait("t", "trh")
            if n == 0:
                e.wait("t", "v", w1_done)
            for m in range(NM):
                ps = (P6 if m % 2 == 0 else P7)[:, 0:512]
                e.wait("t", "s", silu_mark.get((n, m - 2), 0))
                if n > 0 and m < 2:
                    e.wait("t", "s", silu_mark.get((n - 1, NM - 2 + m), 0))
                hTb = hT_c
                for k in range(KD):
                    e.op("t", lambda g, ps=ps, l=w1_b[:, k, 128 * m:128 * (m + 1)],
                         r=hTb[:, k, :], k=k: g.matmul(
                             ps, l, r, start=(k == 0), stop=(k == KD - 1),
                             skip_group_check=True), inc=(k == KD - 1))
                t_m = e.C["t"]
                e.wait("s", "t", t_m)
                if n > 0 and m == 0:
                    e.wait("s", "t", fc2_end[n - 1])
                silu_mark[(n, m)] = sop(lambda g, ps=ps, m=m: g.activation(
                    out=yT[:, m, :], in_=ps, func=AF.Silu,
                    bias=b1col[:, m:m + 1]), inc=True)
            fc1_end[n] = e.C["t"]
            if n + 1 < NC_:
                emit_hT_loads(n + 1)
            e.wait("t", "s", silu_mark[(n, NM - 1)])
            if n == 0:
                e.wait("t", "v", w2_done)
                e.wait("t", "s", B_end_s)
            for mo in range(KD):
                pz = (P012 if mo % 2 == 0 else P345)[:, 0:512]
                e.wait("t", "s", z_s.get((n, mo - 2), 0))
                if n > 0 and mo < 2:
                    e.wait("t", "s", z_s.get((n - 1, KD - 2 + mo), 0))
                for k2 in range(NM):
                    e.op("t", lambda g, pz=pz, l=w2_b[:, k2, 128 * mo:128 * (mo + 1)],
                         r=yT[:, k2, :], k2=k2: g.matmul(
                             pz, l, r, start=(k2 == 0), stop=(k2 == NM - 1),
                             skip_group_check=True), inc=(k2 == NM - 1))
                t_z = e.C["t"]
                e.wait("v", "t", t_z)
                e.wait("v", "trh")
                e.wait("v", "trx")
                z_v[(n, mo)] = vop(lambda g, pz=pz, mo=mo: g.scalar_tensor_tensor(
                    out=pz, in0=pz, scalar=0.5, in1=xnT_c[:, mo, :],
                    op0=OP.mult, op1=OP.add), inc=True)
                sl = mo % 2
                e.wait("s", "v", z_v[(n, mo)])
                e.wait("s", "gts%d" % (mo % 2))
                z_s[(n, mo)] = sop(lambda g, pz=pz, sl=sl, mo=mo: g.activation(
                    out=gt_st[:, sl, :], in_=pz, func=AF.Silu,
                    bias=hb2col[:, mo:mo + 1]), inc=True)
                e.wait("g", "s", z_s[(n, mo)])
                gt_sp[(n, mo)] = dma("g", "gts%d" % (mo % 2), GTD[128 * mo:128 * (mo + 1),
                                            512 * n:512 * (n + 1)], gt_st[:, sl, :])
            fc2_end[n] = e.C["t"]
            if n + 1 < NC_:
                emit_xnT_loads(n + 1)
            e.wait("sy", "gts0"); e.wait("sy", "gts1")
            if n == 0:
                e.wait("sy", "v", w2_done)
            for ii, i in enumerate(tiles):
                sl = i % 2
                e.wait("sy", "v", comb_v.get(i - 2, 0))
                g_tr[n] = dma("sy", "trg", g_st[:, sl, :],
                              GTD[:, 128 * i:128 * (i + 1)], transpose=True)
                m_s = dma("sy", "ld", s_c[:, sl, :], SDr[i])
                m_x = dma("sy", "ld", x_c[:, sl, :], Xr[i])
                e.wait("v", "trg")
                e.wait("v", "ld")
                e.wait("v", "y%d" % (i % 2))
                vop(lambda g, sl=sl: g.tensor_tensor(
                    out=y_st[:, sl, :], in0=s_c[:, sl, :], in1=g_st[:, sl, :],
                    op=OP.mult))
                comb_v[i] = vop(lambda g, sl=sl: g.tensor_tensor(
                    out=y_st[:, sl, :], in0=y_st[:, sl, :], in1=x_c[:, sl, :],
                    op=OP.add), inc=True)
                e.wait("g", "v", comb_v[i])
                y_dma[i] = dma("g", "y%d" % (i % 2),
                               Y.rearrange("(i p) d -> i p d", p=128)[i],
                               y_st[:, sl, :])

        _emit_block(nc, e, blk, sem, None)
    return nc


def _emit_block(nc, e, blk, sem, final_qg):
    for st in DMA_STREAMS:
        e.wait("g", st, e.C[st])
    progs = e.P

    def run(ek):
        def f(eng):
            for (fn, sk, n) in progs[ek]:
                if isinstance(fn, tuple) and fn[0] == "W":
                    eng.wait_ge(sem[fn[1]], fn[2])
                else:
                    instr = fn(eng)
                    if sk is not None:
                        instr.then_inc(sem[sk], n)
        return f

    blk.tensor(run("t"))
    blk.vector(run("v"))
    blk.scalar(run("s"))
    blk.sync(run("sy"))
    blk.gpsimd(run("g"))


def _host_consts():
    p = np.arange(128)[:, None]
    c = np.arange(128)[None, :]
    diag = np.minimum(np.abs(c - p), 16).astype(np.float32)
    corner = np.minimum(np.abs(128 + c - p), 16).astype(np.float32)
    dist = np.concatenate([diag, corner], axis=1)
    zm = (c >= p).astype(ml_dtypes.bfloat16)
    return dist, zm


def kernel(L0_attn_x, L0_attn_Wq, L0_attn_Wk, L0_attn_Wv, L0_attn_Wo, L0_attn_bo,
           L0_attn_rel_emb, L0_ffn_ln_gamma, L0_ffn_ln_beta, L0_ffn_fc1_q,
           L0_ffn_fc1_scales, L0_ffn_fc1_bias, L0_ffn_fc2_q, L0_ffn_fc2_scales,
           L0_ffn_fc2_bias):
    nc = bass.Bass("TRN2", target_bir_lowering=False)
    build(nc)
    dist, zm = _host_consts()
    asC = np.ascontiguousarray
    base = {
        "wq": asC(np.asarray(L0_attn_Wq, np.float32)),
        "wk": asC(np.asarray(L0_attn_Wk, np.float32)),
        "wv": asC(np.asarray(L0_attn_Wv, np.float32)),
        "wo": asC(np.asarray(L0_attn_Wo, np.float32)),
        "bo": asC(np.asarray(L0_attn_bo, np.float32)),
        "rel": asC(np.asarray(L0_attn_rel_emb, np.float32)),
        "gam": asC(np.asarray(L0_ffn_ln_gamma, np.float32)),
        "bet": asC(np.asarray(L0_ffn_ln_beta, np.float32)),
        "q1": asC(np.asarray(L0_ffn_fc1_q, np.int32)),
        "sc1": asC(np.asarray(L0_ffn_fc1_scales, np.float32)),
        "b1c": asC(np.asarray(L0_ffn_fc1_bias, np.float32).reshape(32, 128).T),
        "q2": asC(np.asarray(L0_ffn_fc2_q, np.int32)),
        "sc2": asC(np.asarray(L0_ffn_fc2_scales, np.float32)),
        "b2c": asC(np.asarray(L0_ffn_fc2_bias, np.float32).reshape(8, 128).T),
        "dist": dist,
        "zm": zm,
    }
    xs = np.asarray(L0_attn_x, np.float32)
    in_maps = [dict(base, x=asC(xs[cix])) for cix in range(8)]
    res = run_bass_kernel_spmd(nc, in_maps, core_ids=list(range(8)))
    out = np.stack([np.asarray(r["y"], np.float32) for r in res.results], axis=0)
    return out


# version 14
# speedup vs baseline: 1.0825x; 1.0825x over previous
import numpy as np
import ml_dtypes
import concourse.bass as bass
import concourse.mybir as mybir
from concourse.bass_utils import run_bass_kernel_spmd
from contextlib import ExitStack

F32 = mybir.dt.float32
BF16 = mybir.dt.bfloat16
I32 = mybir.dt.int32
U8 = mybir.dt.uint8
AF = mybir.ActivationFunctionType
OP = mybir.AluOpType
AX = mybir.AxisListType

S, D, H, HD, FF = 4096, 1024, 16, 64, 4096
NT, NC_, KD, NM = 32, 8, 8, 32
EPS = 1e-5
KB = 1024
ENG2SEM = {"t": "t", "v": "v", "s": "s"}


DMA_STREAMS = ("ld", "lds", "bl0", "bl1", "trA0", "trA1", "xnh0", "xnh1", "qkv0", "qkv1", "w",
               "ctxs", "ctxt", "ssp0", "ssp1", "dq", "trh", "trx", "trg", "gts0",
               "gts1", "y0", "y1")


class EM:

    def __init__(self):
        self.P = {k: [] for k in ("t", "v", "s", "sy", "g")}
        self.C = {k: 0 for k in ("t", "v", "s") + DMA_STREAMS}
        self.lastwait = {}

    def op(self, ek, fn, inc=False, sem=None):
        if inc or sem:
            sk = sem or ENG2SEM[ek]
            n = 16 if sk in DMA_STREAMS else 1
            self.C[sk] += n
            self.P[ek].append((fn, sk, n))
            return self.C[sk]
        self.P[ek].append((fn, None, 0))
        return None

    def wait(self, ek, sk, val=None):
        if sk in DMA_STREAMS:
            val = self.C[sk]
        if not val or val <= 0:
            return
        key = (ek, sk)
        if self.lastwait.get(key, -1) >= val:
            return
        self.lastwait[key] = val
        self.P[ek].append((("W", sk, val), None, 0))


def build(nc: bass.Bass, phases=("A", "B", "C"), dbg_spill=False, bopt=None):
    bopt = dict({"nqb": NC_, "nh": H, "do_av": True, "do_wo": True, "wo_stage": 4,
                 "s_nospill": False, "s_copy": False, "par": 0}, **(bopt or {}))
    ap = lambda name, shape, dt: nc.dram_tensor(name, shape, dt, kind="ExternalInput").ap()
    X = ap("x", [S, D], F32)
    Wq, Wk, Wv, Wo = (ap(n, [D, D], F32) for n in ("wq", "wk", "wv", "wo"))
    BO = ap("bo", [D], F32)
    REL = ap("rel", [17], F32)
    GAM, BET = ap("gam", [D], F32), ap("bet", [D], F32)
    Q1 = ap("q1", [D, FF], I32)
    SC1 = ap("sc1", [8, FF], F32)
    B1C = ap("b1c", [128, 32], F32)
    Q2 = ap("q2", [FF, D], I32)
    SC2 = ap("sc2", [32, D], F32)
    B2C = ap("b2c", [128, 8], F32)
    DIST = ap("dist", [128, 256], F32)
    ZM = ap("zm", [128, 128], BF16)

    Y = nc.dram_tensor("y", [S, D], F32, kind="ExternalOutput").ap()
    kscr = "ExternalOutput" if dbg_spill else "Internal"
    scr = lambda name, shape, dt: nc.dram_tensor(name, shape, dt, kind=kscr).ap()
    XN16D = scr("xn16d", [S, D], BF16)
    H16D = scr("h16d", [S, D], BF16)
    QTD = scr("qtd", [D, S], BF16)
    KTD = scr("ktd", [D, S], BF16)
    VAD = scr("vad", [S, H * 65], BF16)
    CTXD = scr("ctxd", [S, D], BF16)
    SD = scr("sd", [S, D], BF16)
    GTD = scr("gtd", [D, S], BF16)

    e = EM()

    def drain_guard(ek, n=2):
        mk = None
        for _ in range(n):
            if ek == "v":
                mk = e.op("v", lambda g: g.tensor_copy(junkc[:, 8:16], junkc[:, 0:8]),
                          inc=True)
            else:
                mk = e.op("s", lambda g: g.activation(out=junkc[:, 8:16],
                                                      in_=junkc[:, 0:8],
                                                      func=AF.Copy), inc=True)
        e.wait(ek, ENG2SEM[ek], mk)

    def parwait(ek):
        for sk in ("t", "v", "s") + DMA_STREAMS:
            if ENG2SEM.get(ek) != sk:
                e.wait(ek, sk, e.C[sk])
    with ExitStack() as es:
        E_ = es.enter_context
        AR = E_(nc.sbuf_tensor("arena", [128, 204 * KB], U8)).ap()

        def carve(off_kb, shape, dt):
            nb = int(np.prod(shape[1:])) * {F32: 4, BF16: 2, I32: 4}[dt]
            off = int(off_kb * KB)
            assert off % 64 == 0 and off + nb <= 204 * KB, (off_kb, shape)
            v = AR[:, off:off + nb].bitcast(dt)
            if len(shape) > 2:
                dims = {f"d{i}": shape[i] for i in range(1, len(shape))}
                pat = ("p (" + " ".join(dims) + ") -> p " + " ".join(dims))
                v = v.rearrange(pat, **{k: dims[k] for k in list(dims)[:-1]})
            return v

        PS = E_(nc.psum_tensor("ps", [128, 4096], F32)).ap() if False else \
            E_(nc.psum_tensor("ps", [128, 4096], F32))
        P012 = PS[:, 0:1536]
        P345 = PS[:, 1536:3072]
        P6 = PS[:, 3072:3584]
        P7 = PS[:, 3584:4096]
        sem = {k: E_(nc.semaphore(f"sem_{k}")) for k in ("t", "v", "s") + DMA_STREAMS}
        blk = E_(nc.Block())

        xnT = carve(0, [128, KD, S], BF16)
        wq_b = carve(64, [128, KD, D], BF16)
        wk_b = carve(80, [128, KD, D], BF16)
        wv_b = carve(96, [128, KD, D], BF16)
        x_ring = carve(112, [128, 4, D], F32)
        sq_scr = carve(128, [128, D], F32)
        ht_scr = carve(132, [128, D], F32)
        xn_st = carve(136, [128, 2, D], BF16)
        h_st = carve(140, [128, 2, D], BF16)
        q_st = carve(144, [128, 2, 512], BF16)
        k_st = carve(146, [128, 2, 512], BF16)
        v_st = carve(148, [128, 2, H, 65], BF16)
        gam_b = carve(153, [128, D], F32)
        bet_b = carve(157, [128, D], F32)
        colsA = carve(161, [128, 64], F32)
        bo_b = carve(187, [128, D], F32)
        B_dc = carve(191, [128, 256], F32)
        zm_sb = carve(192, [128, 128], BF16)
        rel_bc = carve(192.25, [128, 17], F32)
        r16c = carve(192.375, [128, 1], F32)
        ri16c = carve(192.4375, [128, 1], F32)
        a16c = carve(192.5, [128, 1], F32)
        epsc = carve(192.5625, [128, 1], F32)
        tmpB = carve(193, [128, 256], F32)
        tmpB2 = carve(195, [128, 256], F32)
        rlc = carve(196, [128, 8], F32)
        kT = carve(0, [128, KD, S], BF16)
        v_sb = carve(64, [128, NT, H * 65], BF16)
        qT = carve(129, [128, 2, KD, 512], BF16)
        E_sb = carve(145, [128, 2, 2048], BF16)
        wo_b = carve(153, [128, KD, D], BF16)
        ctx_sb = carve(169, [128, 4, D], BF16)
        ctxT = carve(177, [128, KD, 512], BF16)
        s_st = carve(185, [128, 2, 512], BF16)
        w1_b = carve(0, [128, 8, FF], BF16)
        w2_b = carve(64, [128, 32, D], BF16)
        yT = carve(128, [128, NM, 512], BF16)
        q1q = carve(129, [128, 2, 1024], I32)
        sc1row_q = carve(137, [128, 2, 1024], F32)
        ones1 = carve(194, [128, 128], F32)
        hT_c = carve(160, [128, KD, 512], BF16)
        xnT_c = carve(168, [128, KD, 512], BF16)
        x_c = carve(176, [128, 2, D], F32)
        s_c = carve(184, [128, 2, D], BF16)
        gt_st = carve(188, [128, 2, 512], BF16)
        g_st = carve(190, [128, 2, D], BF16)
        y_st = carve(194, [128, 2, D], F32)
        b1col = carve(202, [128, 32], F32)
        junkc = carve(203, [128, 16], F32)
        hb2col = carve(202.25, [128, 8], F32)

        def dma(ek, st, out, in_, **kw):
            return e.op(ek, lambda g, out=out, in_=in_, kw=kw: g.dma_start(out=out, in_=in_, **kw), sem=st)

        def vop(fn, inc=False):
            return e.op("v", fn, inc=inc)

        def sop(fn, inc=False):
            return e.op("s", fn, inc=inc)

        m_wcast = None
        for (w_sb, Wsrc) in ((wq_b, Wq), (wk_b, Wk), (wv_b, Wv)):
            Wr = Wsrc.rearrange("(k p) n -> k p n", p=128)
            for k in range(KD):
                m_wcast = dma("g", "w", w_sb[:, k, :], Wr[k])
        m_gb = dma("sy", "ld", gam_b[:], GAM.unsqueeze(0).to_broadcast((128, D)))
        m_bb = dma("sy", "ld", bet_b[:], BET.unsqueeze(0).to_broadcast((128, D)))
        dma("sy", "ld", bo_b[:], BO.unsqueeze(0).to_broadcast((128, D)))
        dma("sy", "ld", rel_bc[:], REL.unsqueeze(0).to_broadcast((128, 17)))
        m_r16 = dma("sy", "ld", r16c[:], REL[16:17].unsqueeze(0).to_broadcast((128, 1)))
        m_dist = dma("sy", "ld", tmpB2[:], DIST)
        m_zm = dma("sy", "ld", zm_sb[:], ZM)
        m_b1c = dma("sy", "ld", b1col[:], B1C)
        m_b2craw = dma("sy", "ld", hb2col[:], B2C)

        e.wait("v", "ld")
        vop(lambda g: g.memset(epsc[:], EPS), inc=True)
        vop(lambda g: g.memset(v_st[:, 0, :, 64:65], 1.0), inc=True)
        vop(lambda g: g.memset(v_st[:, 1, :, 64:65], 1.0), inc=True)
        vop(lambda g: g.memset(ones1[0:1, :], 1.0), inc=True)
        vop(lambda g: g.reciprocal(ri16c[:], r16c[:]), inc=True)
        mk0 = vop(lambda g: g.tensor_scalar(out=a16c[:], in0=r16c[:], scalar1=0.125,
                                            scalar2=None, op0=OP.mult), inc=True)
        e.wait("v", "v", mk0)
        mk0 = vop(lambda g: g.tensor_scalar(out=rel_bc[:], in0=rel_bc[:],
                                            scalar1=ri16c[:], scalar2=None,
                                            op0=OP.mult), inc=True)
        e.wait("v", "v", mk0)
        e.wait("v", "ld")
        for r in range(17):
            dst = B_dc if r == 0 else tmpB
            mk0 = vop(lambda g, r=r, dst=dst: g.tensor_scalar(
                out=dst[:], in0=tmpB2[:], scalar1=float(r), scalar2=rel_bc[:, r:r + 1],
                op0=OP.is_equal, op1=OP.mult), inc=True)
            e.wait("v", "v", mk0)
            if r > 0:
                mk0 = vop(lambda g: g.tensor_tensor(out=B_dc[:], in0=B_dc[:],
                                                    in1=tmpB[:], op=OP.add), inc=True)
                e.wait("v", "v", mk0)
        e.wait("v", "ld")
        m_setup = vop(lambda g: g.tensor_scalar(out=hb2col[:], in0=hb2col[:],
                                                scalar1=0.5, scalar2=None, op0=OP.mult),
                      inc=True)
        v_setup_end = m_setup
        s_prev_chunk = [0]

        Xr = X.rearrange("(i p) d -> i p d", p=128)
        XN16r = XN16D.rearrange("(i p) d -> i p d", p=128)
        H16r = H16D.rearrange("(i p) d -> i p d", p=128)
        QTr = QTD.rearrange("(m p) q -> m p q", p=128)
        KTr = KTD.rearrange("(m p) q -> m p q", p=128)
        VAr = VAD.rearrange("(i p) c -> i p c", p=128)

        m_xload = [0] * NT
        m_a1p1 = [0] * NT
        m_sq = [0] * NT
        m_hv = [0] * NT
        m_xnv = [0] * NT
        m_xnsp = [0] * NT
        m_hsp = [0] * NT
        m_tr = [0] * NC_
        qk_ev = {}
        qk_sp = {}
        v_ev = {}
        v_sp = {}
        last_bank_ev = {}
        C_ = lambda i, j: colsA[:, (i % 4) * 16 + j:(i % 4) * 16 + j + 1]

        def emit_trA(n):
            e.wait("sy", "xnh%d" % (n % 2))
            for k in range(KD):
                m_tr[n] = dma("sy", "trA%d" % (n % 2), xnT[:, k, 512 * n:512 * (n + 1)],
                              XN16D[512 * n:512 * (n + 1), 128 * k:128 * (k + 1)],
                              transpose=True)

        def emit_A1(n):
            tiles = list(range(4 * n, 4 * n + 4))
            for i in tiles:
                if i >= 4:
                    e.wait("sy", "v", m_hv[i - 4])
                m_xload[i] = dma("sy", "ld", x_ring[:, i % 4, :], Xr[i])
            if n >= 1:
                emit_trA(n - 1)
            e.wait("v", "ld")
            e.wait("v", "v", v_setup_end)
            if n > 0:
                e.wait("v", "s", s_prev_chunk[0])
            for i in tiles:
                xt = x_ring[:, i % 4, :]
                e.wait("v", "ld")
                vop(lambda g, xt=xt: g.tensor_tensor(out=sq_scr[:], in0=xt, in1=xt,
                                                     op=OP.mult))
                vop(lambda g, i=i: g.tensor_reduce(out=C_(i, 0), in_=sq_scr[:],
                                                   axis=AX.X, op=OP.add), inc=True)
                mk = vop(lambda g, xt=xt, i=i: g.tensor_reduce(
                    out=C_(i, 1), in_=xt, axis=AX.X, op=OP.add), inc=True)
            e.wait("v", "v", mk)
            for i in tiles:
                vop(lambda g, i=i: g.tensor_scalar(out=C_(i, 2), in0=C_(i, 1),
                                                   scalar1=1.0 / D, scalar2=None,
                                                   op0=OP.mult), inc=True)
                mk = vop(lambda g, i=i: g.tensor_scalar(
                    out=C_(i, 4), in0=C_(i, 0), scalar1=1.0 / D, scalar2=epsc[:],
                    op0=OP.mult, op1=OP.add), inc=True)
            e.wait("v", "v", mk)
            for i in tiles:
                mk = vop(lambda g, i=i: g.tensor_tensor(out=C_(i, 3), in0=C_(i, 2),
                                                        in1=C_(i, 2), op=OP.mult),
                         inc=True)
            e.wait("v", "v", mk)
            for i in tiles:
                mk = vop(lambda g, i=i: g.tensor_tensor(out=C_(i, 5), in0=C_(i, 4),
                                                        in1=C_(i, 3), op=OP.subtract),
                         inc=True)
            e.wait("s", "v", mk)
            for i in tiles:
                mks = sop(lambda g, i=i: g.activation(out=C_(i, 7), in_=C_(i, 5),
                                                      func=AF.Sqrt), inc=True)
            e.wait("v", "s", mks)
            for i in tiles:
                vop(lambda g, i=i: g.reciprocal(C_(i, 8), C_(i, 7)), inc=True)
                mk = vop(lambda g, i=i: g.reciprocal(C_(i, 6), C_(i, 5)), inc=True)
            e.wait("s", "v", mk)
            for i in tiles:
                mks = sop(lambda g, i=i: g.activation(out=C_(i, 9), in_=C_(i, 6),
                                                      func=AF.Copy, scale=-EPS,
                                                      bias=1.0 + EPS), inc=True)
            e.wait("s", "s", mks)
            for i in tiles:
                mks = sop(lambda g, i=i: g.activation(out=C_(i, 10), in_=C_(i, 9),
                                                      func=AF.Sqrt), inc=True)
            s_prev_chunk[0] = mks
            e.wait("v", "s", mks)
            for i in tiles:
                mk = vop(lambda g, i=i: g.reciprocal(C_(i, 11), C_(i, 10)), inc=True)
            e.wait("v", "v", mk)
            for i in tiles:
                mk = vop(lambda g, i=i: g.tensor_tensor(out=C_(i, 13), in0=C_(i, 8),
                                                        in1=C_(i, 11), op=OP.mult),
                         inc=True)
            e.wait("v", "v", mk)
            for i in tiles:
                vop(lambda g, i=i: g.tensor_tensor(out=C_(i, 12), in0=C_(i, 2),
                                                   in1=C_(i, 8), op=OP.mult), inc=True)
                mk = vop(lambda g, i=i: g.tensor_tensor(out=C_(i, 14), in0=C_(i, 2),
                                                        in1=C_(i, 13), op=OP.mult),
                         inc=True)
            e.wait("v", "v", mk)
            for i in tiles:
                xt = x_ring[:, i % 4, :]
                sl = i % 2
                if i >= 2:
                    e.wait("v", "xnh%d" % (((i - 2) // 4) % 2))
                m_xnv[i] = vop(lambda g, xt=xt, i=i, sl=sl: g.tensor_scalar(
                    out=xn_st[:, sl, :], in0=xt, scalar1=C_(i, 8), scalar2=C_(i, 12),
                    op0=OP.mult, op1=OP.subtract), inc=True)
                vop(lambda g, xt=xt, i=i: g.tensor_scalar(
                    out=ht_scr[:], in0=xt, scalar1=C_(i, 13), scalar2=C_(i, 14),
                    op0=OP.mult, op1=OP.subtract))
                vop(lambda g: g.tensor_tensor(out=ht_scr[:], in0=ht_scr[:],
                                              in1=gam_b[:], op=OP.mult))
                m_hv[i] = vop(lambda g, sl=sl: g.tensor_tensor(
                    out=h_st[:, sl, :], in0=ht_scr[:], in1=bet_b[:], op=OP.add),
                    inc=True)
                e.wait("g", "v", m_xnv[i])
                m_xnsp[i] = dma("g", "xnh%d" % ((i // 4) % 2), XN16r[i], xn_st[:, sl, :])
                e.wait("g", "v", m_hv[i])
                m_hsp[i] = dma("g", "xnh%d" % ((i // 4) % 2), H16r[i], h_st[:, sl, :])

        def emit_A2(n):
            tiles = list(range(4 * n, 4 * n + 4))
            e.wait("t", "trA%d" % (n % 2))
            if n == 0:
                e.wait("t", "w")
                e.wait("t", "v", m_setup)
                e.wait("s", "v", m_setup)
            nwin = slice(512 * n, 512 * (n + 1))
            for (nm, w_sb, stg, dstD) in (("q", wq_b, q_st, QTr), ("k", wk_b, k_st, KTr)):
                for m in range(KD):
                    gidx = n * KD + m
                    ps = (P6 if m % 2 == 0 else P7)[:, 0:512]
                    e.wait("t", "s", last_bank_ev.get(m % 2, 0))
                    for k in range(KD):
                        e.op("t", lambda g, ps=ps, w=w_sb[:, k, 128 * m:128 * (m + 1)],
                             r=xnT[:, k, nwin], k=k: g.matmul(
                                 ps, w, r, start=(k == 0), stop=(k == KD - 1),
                                 skip_group_check=True),
                             inc=(k == KD - 1))
                    tmark = e.C["t"]
                    sl = m % 2
                    e.wait("s", "t", tmark)
                    e.wait("s", "qkv%d" % (gidx % 2))
                    if nm == "q":
                        mev = sop(lambda g, ps=ps, sl=sl: g.activation(
                            out=q_st[:, sl, :], in_=ps, func=AF.Copy), inc=True)
                        qk_sp[(nm, gidx)] = dma("s", "qkv%d" % (gidx % 2), dstD[m][:, nwin], q_st[:, sl, :])
                    else:
                        mev = sop(lambda g, ps=ps, sl=sl: g.activation(
                            out=k_st[:, sl, :], in_=ps, func=AF.Copy,
                            scale=a16c[:]), inc=True)
                        qk_sp[(nm, gidx)] = dma("s", "qkv%d" % (gidx % 2), dstD[m][:, nwin], k_st[:, sl, :])
                    qk_ev[(nm, gidx)] = mev
                    last_bank_ev[m % 2] = mev
            for i in tiles:
                pv = (P012 if i % 2 == 0 else P345)[:, 0:1024]
                e.wait("t", "s", v_ev.get(i - 2, 0))
                for k in range(KD):
                    for hh in range(2):
                        e.op("t", lambda g, pv=pv, hh=hh,
                             l=xnT[:, k, 128 * i:128 * (i + 1)],
                             r=wv_b[:, k, 512 * hh:512 * (hh + 1)], k=k: g.matmul(
                                 pv[:, 512 * hh:512 * (hh + 1)], l, r,
                                 start=(k == 0), stop=(k == KD - 1),
                                 skip_group_check=True),
                             inc=(k == KD - 1 and hh == 1))
                tmark = e.C["t"]
                sl = i % 2
                e.wait("s", "t", tmark)
                e.wait("s", "qkv%d" % (i % 2))
                v_ev[i] = sop(lambda g, pv=pv, sl=sl: g.activation(
                    out=v_st[:, sl, :, 0:64],
                    in_=pv.rearrange("p (h e) -> p h e", h=H), func=AF.Copy),
                    inc=True)
                e.wait("s", "v", m_setup)
                v_sp[i] = dma("s", "qkv%d" % (i % 2), VAr[i], v_st[:, sl].rearrange("p h e -> p (h e)"))

        for n in range(NC_ + 2):
            if n == NC_:
                emit_trA(NC_ - 1)
            if n >= 2:
                emit_A2(n - 2)
            if n < NC_:
                emit_A1(n)

        A_end_t = e.C["t"]
        A_end_v = e.C["v"]

        if "B" not in phases:
            _emit_block(nc, e, blk, sem, None)
            return nc

        e.wait("g", "v", A_end_v)
        wo_cast = None
        WoR = Wo.rearrange("(k p) n -> k p n", p=128)
        for k in range(KD):
            wo_cast = dma("g", "w", wo_b[:, k, :], WoR[k])

        e.wait("t", "v", A_end_v)
        e.wait("v", "ld")
        KTD_r = KTD.rearrange("(t p) q -> p t q", p=128)
        QTD_r = QTD.rearrange("(t p) q -> p t q", p=128)
        VAD_r = VAD.rearrange("(t p) c -> p t c", p=128)
        CTXr = CTXD.rearrange("(i p) d -> i p d", p=128)
        SDr = SD.rearrange("(i p) d -> i p d", p=128)

        exp_half = {}
        av_half = {}
        B_marks = {"scores": 0, "av": 0}
        last_scores_qb = [0] * NC_
        ctx_ev = {}
        ctx_sp = [0] * NC_
        ctxT_tr = [0] * NC_
        wo_end = [0] * NC_
        s_ev = {}
        s_sp = {}
        ci_glob = [0]

        def emit_wo_group(qbw, gidx):
            qs, half = gidx // 2, gidx % 2
            pso = (P6 if gidx % 2 == 0 else P7)[:, 0:512]
            prev_sev = s_ev.get((qbw, gidx - 2), s_ev.get((qbw - 1, gidx + 6), 0))
            e.wait("t", "v", prev_sev)
            e.wait("t", "ctxt")
            if qbw == 0 and gidx == 0:
                e.wait("t", "w")
            for k in range(KD):
                e.op("t", lambda g, pso=pso,
                     l=ctxT[:, k, 128 * qs:128 * (qs + 1)],
                     r=wo_b[:, k, 512 * half:512 * (half + 1)], k=k: g.matmul(
                         pso, l, r, start=(k == 0),
                         stop=(k == KD - 1), skip_group_check=True),
                     inc=(k == KD - 1))
            t_o = e.C["t"]
            sl = gidx % 2
            e.wait("v", "t", t_o)
            e.wait("v", "ssp%d" % (gidx % 2))
            s_ev[(qbw, gidx)] = vop(lambda g, sl=sl, half=half, pso=pso:
                                    g.tensor_tensor(
                                        out=s_st[:, sl, :], in0=pso,
                                        in1=bo_b[:, 512 * half:512 * (half + 1)],
                                        op=OP.add), inc=True)
            e.wait("g", "v", s_ev[(qbw, gidx)])
            s_sp[(qbw, gidx)] = dma(
                "g", "ssp%d" % (gidx % 2),
                SDr[4 * qbw + qs][:, 512 * half:512 * (half + 1)],
                s_st[:, sl, :])
            if gidx == 7:
                wo_end[qbw] = e.C["t"]

        def emit_b_loads(qb):
            e.wait("sy", "t", A_end_t)
            e.wait("sy", "xnh0"); e.wait("sy", "xnh1")
            e.wait("sy", "qkv0"); e.wait("sy", "qkv1")
            st = "bl%d" % (qb % 2)
            dma("sy", st, kT[:, :, 512 * qb:512 * (qb + 1)],
                KTD_r[:, :, 512 * qb:512 * (qb + 1)])
            dma("sy", st, v_sb[:, 4 * qb:4 * qb + 4, :],
                VAD_r[:, 4 * qb:4 * qb + 4, :])
            if qb >= 2:
                e.wait("sy", "t", last_scores_qb[qb - 2])
            dma("sy", st, qT[:, qb % 2], QTD_r[:, :, 512 * qb:512 * (qb + 1)])

        emit_b_loads(0)
        for qb in range(bopt["nqb"]):
            J = 4 * qb + 4

            jl = list(range(J))

            def emit_scores_half(u, j0, hp, jj):
                kt_tile = hp
                ebuf = E_sb[:, u % 2]
                j = j0 + jj
                sd_ = j - 4 * qb
                if jj == 0:
                    e.wait("t", "bl%d" % (qb % 2))
                e.wait("t", "s", exp_half.get((u - 1, jj), 0))
                for par in range(2):
                    poff = 64 * par
                    def smm(g, j=j, sd_=sd_, jj=jj, par=par, kt_tile=kt_tile,
                            poff=poff, qb2=qb % 2):
                        l_ap = kT[poff:poff + 64, kt_tile, 128 * j:128 * (j + 1)]
                        lo = 1024 * jj + 512 * par
                        if sd_ < 0:
                            r_ap = qT[poff:poff + 64, qb2, kt_tile, 0:512]
                            o_ap = PS[:, lo:lo + 512]
                        else:
                            r_ap = qT[poff:poff + 64, qb2, kt_tile, 128 * sd_:512]
                            o_ap = PS[:, lo + 128 * sd_:lo + 512]
                        return g.matmul(o_ap, l_ap, r_ap, start=True, stop=True,
                                        skip_group_check=True)
                    e.op("t", smm, inc=(par == 1))
                t_sc = e.C["t"]
                B_marks["scores"] = t_sc
                v_band = None
                breg = None
                if sd_ == -1:
                    breg = (0, 128, 128)
                elif 0 <= sd_ <= 3:
                    breg = (128 * sd_, min(256, 512 - 128 * sd_), 0)
                if breg is not None:
                    o0, w, boff = breg
                    e.wait("v", "t", t_sc)
                    for par in range(2):
                        lo = 1024 * jj + 512 * par + o0
                        v_band = vop(lambda g, lo=lo, w=w, boff=boff:
                                     g.tensor_tensor(out=PS[:, lo:lo + w],
                                                     in0=PS[:, lo:lo + w],
                                                     in1=B_dc[:, boff:boff + w],
                                                     op=OP.mult),
                                     inc=True)
                e.wait("s", "t", t_sc)
                if v_band:
                    e.wait("s", "v", v_band)
                e.wait("s", "t", av_half.get((u - 2, jj), 0))
                exp_half[(u, jj)] = sop(lambda g, ebuf=ebuf, jj=jj:
                                        g.activation(
                                            out=ebuf[0:128, 1024 * jj:1024 * jj + 1024],
                                            in_=PS[:, 1024 * jj:1024 * jj + 1024],
                                            func=AF.Exp), inc=True)
                v_z = None
                if 0 <= sd_ <= 3:
                    e.wait("v", "s", exp_half[(u, jj)])
                    for par in range(2):
                        col = 1024 * jj + 512 * par + 128 * sd_
                        v_z = vop(lambda g, ebuf=ebuf, col=col: g.tensor_tensor(
                            out=ebuf[0:128, col:col + 128],
                            in0=ebuf[0:128, col:col + 128], in1=zm_sb[:],
                            op=OP.mult), inc=True)
                zv_mark[(u, jj)] = v_z

            def emit_av_half(u, j0, hp, jj, first):
                ebuf = E_sb[:, u % 2]
                e.wait("t", "s", exp_half[(u, jj)])
                if zv_mark.get((u, jj)):
                    e.wait("t", "v", zv_mark[(u, jj)])
                if first:
                    prev = ctx_ev.get((qb, hp - 1), 0) if hp > 0 else \
                        (ctx_ev.get((qb - 1, H // 2 - 1), 0) if qb > 0 else 0)
                    e.wait("t", "v", prev)
                j = j0 + jj
                sd_ = j - 4 * qb
                for par in range(2):
                    h = 2 * hp + par
                    cbase = 2048 + 512 * par
                    for qs in range(max(0, sd_), 4):
                        def av(g, jj=jj, j=j, qs=qs, par=par, h=h, cbase=cbase,
                               ebuf=ebuf, J=J):
                            return g.matmul(
                                PS[:, cbase + 65 * qs:cbase + 65 * qs + 65],
                                ebuf[0:128, 1024 * jj + 512 * par + 128 * qs:
                                     1024 * jj + 512 * par + 128 * qs + 128],
                                v_sb[:, j, 65 * h:65 * (h + 1)],
                                start=(j == 0 and qs == 0),
                                stop=(j == J - 1 and qs == 3),
                                skip_group_check=True)
                        e.op("t", av, inc=(par == 1 and qs == 3))
                av_half[(u, jj)] = e.C["t"]
                B_marks["av"] = av_half[(u, jj)]

            zv_mark = {}
            for hp in range(H // 2):
                if hp == 1 and qb + 1 < bopt["nqb"]:
                    emit_b_loads(qb + 1)
                pend = None
                first = True
                for idx, j0 in enumerate(range(0, J, 2)):
                    u = ci_glob[0]
                    ci_glob[0] += 1
                    emit_scores_half(u, j0, hp, 0)
                    if pend is not None:
                        emit_av_half(pend[0], pend[1], hp, 0, first)
                    emit_scores_half(u, j0, hp, 1)
                    if pend is not None:
                        emit_av_half(pend[0], pend[1], hp, 1, False)
                        first = False
                    pend = (u, j0)
                    if idx == 0 and hp >= 1 and qb > 0 and bopt["do_wo"] \
                            and bopt["nh"] == H:
                        emit_wo_group(qb - 1, hp)
                emit_av_half(pend[0], pend[1], hp, 0, first)
                emit_av_half(pend[0], pend[1], hp, 1, False)
                if hp == 0 and qb > 0 and bopt["do_wo"] and bopt["nh"] == H:
                    emit_wo_group(qb - 1, hp)
                t_av = av_half[(pend[0], 1)]
                e.wait("v", "t", t_av)
                if hp == 0 and qb > 0:
                    e.wait("v", "ctxs")
                mkr = None
                for par in range(2):
                    cbase = 2048 + 512 * par
                    for qs in range(4):
                        mkr = vop(lambda g, qs=qs, par=par, cbase=cbase: g.reciprocal(
                            rlc[:, 4 * par + qs:4 * par + qs + 1],
                            PS[:, cbase + 65 * qs + 64:cbase + 65 * qs + 65]),
                            inc=True)
                e.wait("v", "v", mkr)
                for par in range(2):
                    h = 2 * hp + par
                    cbase = 2048 + 512 * par
                    for qs in range(4):
                        ctx_ev[(qb, hp)] = vop(lambda g, qs=qs, h=h, par=par,
                                               cbase=cbase: g.tensor_scalar(
                            out=ctx_sb[:, qs, 64 * h:64 * (h + 1)],
                            in0=PS[:, cbase + 65 * qs:cbase + 65 * qs + 64],
                            scalar1=rlc[:, 4 * par + qs:4 * par + qs + 1],
                            scalar2=None, op0=OP.mult),
                            inc=(par == 1 and qs == 3))
            last_scores_qb[qb] = e.C["t"]
            if not (bopt["do_av"] and bopt["do_wo"] and bopt["nh"] == H):
                continue
            e.wait("g", "v", ctx_ev[(qb, H // 2 - 1)])
            CTXr = CTXD.rearrange("(i p) d -> i p d", p=128)
            for qs in range(4):
                ctx_sp[qb] = dma("g", "ctxs", CTXr[4 * qb + qs], ctx_sb[:, qs, :])
            e.wait("sy", "ctxs")
            if qb > 0:
                e.wait("sy", "t", wo_end[qb - 1])
            for k in range(KD):
                ctxT_tr[qb] = dma("sy", "ctxt", ctxT[:, k, :],
                                  CTXD[512 * qb:512 * (qb + 1), 128 * k:128 * (k + 1)],
                                  transpose=True)
        if bopt["do_av"] and bopt["do_wo"] and bopt["nh"] == H and bopt["nqb"] == NC_:
            for gidx in range(8):
                emit_wo_group(NC_ - 1, gidx)

        B_end_t = e.C["t"]
        B_end_v = e.C["v"]
        B_end_s = e.C["s"]

        if "C" not in phases:
            _emit_block(nc, e, blk, sem, None)
            return nc

        Q1r = Q1.rearrange("(g p) f -> g p f", p=128)
        Q2r = Q2.rearrange("(g p) d -> g p d", p=128)
        B_scores_final = B_marks["scores"]
        B_last_av = B_marks["av"]
        e.wait("sy", "t", B_scores_final)
        e.wait("t", "s", B_end_s)
        all_stt = []
        bc_idx = 0

        def dq_unit(w_dst, q_src_ap, sc_row_slice, w2=False):
            psb = PS[:, 512 * (bc_idx % 2):512 * (bc_idx % 2) + 512]
            if bc_idx >= 2:
                e.wait("t", "v", all_stt[bc_idx - 2])
            e.op("t", lambda g, psb=psb, r=sc_row_slice: g.matmul(
                psb, ones1[0:1, :], r, start=True, stop=True,
                skip_group_check=True), inc=True)
            tbc = e.C["t"]
            e.wait("v", "t", tbc)
            all_stt.append(vop(lambda g, w_dst=w_dst, q=q_src_ap, psb=psb:
                               g.scalar_tensor_tensor(
                                   out=w_dst, in0=q, scalar=-128.0, in1=psb,
                                   op0=OP.add, op1=OP.mult), inc=True))

        for gix in range(8):
            for qt in range(4):
                it = gix * 4 + qt
                sl = it % 2
                if it >= 2:
                    e.wait("sy", "v", all_stt[2 * (it - 2) + 1])
                mq = dma("sy", "dq", q1q[:, sl, :], Q1r[gix][:, 1024 * qt:1024 * (qt + 1)])
                msc = dma("sy", "dq", sc1row_q[0:1, sl, :],
                          SC1[gix:gix + 1, 1024 * qt:1024 * (qt + 1)])
                e.wait("t", "dq")
                e.wait("v", "dq")
                for c in range(2):
                    dq_unit(
                        w1_b[:, gix, 1024 * qt + 512 * c:1024 * qt + 512 * (c + 1)],
                        q1q[:, sl, 512 * c:512 * (c + 1)],
                        sc1row_q[0:1, sl, 512 * c:512 * (c + 1)])
                    bc_idx += 1
        w1_done = all_stt[-1]
        e.wait("v", "t", B_last_av)
        e.wait("sy", "v", w1_done)
        n_w1 = len(all_stt)
        for g2 in range(32):
            sl = g2 % 2
            if g2 >= 2:
                e.wait("sy", "v", all_stt[n_w1 + 2 * (g2 - 2) + 1])
            mq = dma("sy", "dq", q1q[:, sl, :], Q2r[g2])
            msc = dma("sy", "dq", sc1row_q[0:1, sl, :], SC2[g2:g2 + 1, :])
            e.wait("t", "dq")
            e.wait("v", "dq")
            for c in range(2):
                dq_unit(w2_b[:, g2, 512 * c:512 * (c + 1)],
                        q1q[:, sl, 512 * c:512 * (c + 1)],
                        sc1row_q[0:1, sl, 512 * c:512 * (c + 1)], w2=True)
                bc_idx += 1
        w2_done = all_stt[-1]

        fc1_end = [0] * NC_
        fc2_end = [0] * NC_
        silu_mark = {}
        z_v = {}
        z_s = {}
        gt_sp = {}
        g_tr = [0] * NC_
        comb_v = {}
        y_dma = {}
        hT_ready = [0] * NC_
        xnT_ready = [0] * NC_
        sx_ready = {}

        def emit_hT_loads(n):
            if n == 0:
                e.wait("sy", "v", w1_done)
                e.wait("sy", "ctxs"); e.wait("sy", "ssp0"); e.wait("sy", "ssp1")
                e.wait("s", "ld")
            else:
                e.wait("sy", "t", fc1_end[n - 1])
            for k in range(KD):
                hT_ready[n] = dma("sy", "trh", hT_c[:, k, :],
                                  H16D[512 * n:512 * (n + 1), 128 * k:128 * (k + 1)],
                                  transpose=True)

        def emit_xnT_loads(n):
            if n > 0:
                e.wait("sy", "v", z_v.get((n - 1, 7), 0))
            for k in range(KD):
                xnT_ready[n] = dma("sy", "trx", xnT_c[:, k, :],
                                   XN16D[512 * n:512 * (n + 1), 128 * k:128 * (k + 1)],
                                   transpose=True)
        emit_hT_loads(0)
        emit_xnT_loads(0)
        for n in range(NC_):
            tiles = list(range(4 * n, 4 * n + 4))
            e.wait("t", "trh")
            if n == 0:
                e.wait("t", "v", w1_done)
            for m in range(NM):
                ps = (P6 if m % 2 == 0 else P7)[:, 0:512]
                e.wait("t", "s", silu_mark.get((n, m - 2), 0))
                if n > 0 and m < 2:
                    e.wait("t", "s", silu_mark.get((n - 1, NM - 2 + m), 0))
                hTb = hT_c
                for k in range(KD):
                    e.op("t", lambda g, ps=ps, l=w1_b[:, k, 128 * m:128 * (m + 1)],
                         r=hTb[:, k, :], k=k: g.matmul(
                             ps, l, r, start=(k == 0), stop=(k == KD - 1),
                             skip_group_check=True), inc=(k == KD - 1))
                t_m = e.C["t"]
                e.wait("s", "t", t_m)
                if n > 0 and m == 0:
                    e.wait("s", "t", fc2_end[n - 1])
                silu_mark[(n, m)] = sop(lambda g, ps=ps, m=m: g.activation(
                    out=yT[:, m, :], in_=ps, func=AF.Silu,
                    bias=b1col[:, m:m + 1]), inc=True)
            fc1_end[n] = e.C["t"]
            if n + 1 < NC_:
                emit_hT_loads(n + 1)
            e.wait("t", "s", silu_mark[(n, NM - 1)])
            if n == 0:
                e.wait("t", "v", w2_done)
                e.wait("t", "s", B_end_s)
            for mo in range(KD):
                pz = (P012 if mo % 2 == 0 else P345)[:, 0:512]
                e.wait("t", "s", z_s.get((n, mo - 2), 0))
                if n > 0 and mo < 2:
                    e.wait("t", "s", z_s.get((n - 1, KD - 2 + mo), 0))
                for k2 in range(NM):
                    e.op("t", lambda g, pz=pz, l=w2_b[:, k2, 128 * mo:128 * (mo + 1)],
                         r=yT[:, k2, :], k2=k2: g.matmul(
                             pz, l, r, start=(k2 == 0), stop=(k2 == NM - 1),
                             skip_group_check=True), inc=(k2 == NM - 1))
                t_z = e.C["t"]
                e.wait("v", "t", t_z)
                e.wait("v", "trh")
                e.wait("v", "trx")
                z_v[(n, mo)] = vop(lambda g, pz=pz, mo=mo: g.scalar_tensor_tensor(
                    out=pz, in0=pz, scalar=0.5, in1=xnT_c[:, mo, :],
                    op0=OP.mult, op1=OP.add), inc=True)
                sl = mo % 2
                e.wait("s", "v", z_v[(n, mo)])
                e.wait("s", "gts%d" % (mo % 2))
                z_s[(n, mo)] = sop(lambda g, pz=pz, sl=sl, mo=mo: g.activation(
                    out=gt_st[:, sl, :], in_=pz, func=AF.Silu,
                    bias=hb2col[:, mo:mo + 1]), inc=True)
                e.wait("g", "s", z_s[(n, mo)])
                gt_sp[(n, mo)] = dma("g", "gts%d" % (mo % 2), GTD[128 * mo:128 * (mo + 1),
                                            512 * n:512 * (n + 1)], gt_st[:, sl, :])
            fc2_end[n] = e.C["t"]
            if n + 1 < NC_:
                emit_xnT_loads(n + 1)
            e.wait("sy", "gts0"); e.wait("sy", "gts1")
            if n == 0:
                e.wait("sy", "v", w2_done)
            for ii, i in enumerate(tiles):
                sl = i % 2
                e.wait("sy", "v", comb_v.get(i - 2, 0))
                g_tr[n] = dma("sy", "trg", g_st[:, sl, :],
                              GTD[:, 128 * i:128 * (i + 1)], transpose=True)
                m_s = dma("sy", "ld", s_c[:, sl, :], SDr[i])
                m_x = dma("sy", "ld", x_c[:, sl, :], Xr[i])
                e.wait("v", "trg")
                e.wait("v", "ld")
                e.wait("v", "y%d" % (i % 2))
                vop(lambda g, sl=sl: g.tensor_tensor(
                    out=y_st[:, sl, :], in0=s_c[:, sl, :], in1=g_st[:, sl, :],
                    op=OP.mult))
                comb_v[i] = vop(lambda g, sl=sl: g.tensor_tensor(
                    out=y_st[:, sl, :], in0=y_st[:, sl, :], in1=x_c[:, sl, :],
                    op=OP.add), inc=True)
                e.wait("g", "v", comb_v[i])
                y_dma[i] = dma("g", "y%d" % (i % 2),
                               Y.rearrange("(i p) d -> i p d", p=128)[i],
                               y_st[:, sl, :])

        _emit_block(nc, e, blk, sem, None)
    return nc


def _emit_block(nc, e, blk, sem, final_qg):
    for st in DMA_STREAMS:
        e.wait("g", st, e.C[st])
    progs = e.P

    def run(ek):
        def f(eng):
            for (fn, sk, n) in progs[ek]:
                if isinstance(fn, tuple) and fn[0] == "W":
                    eng.wait_ge(sem[fn[1]], fn[2])
                else:
                    instr = fn(eng)
                    if sk is not None:
                        instr.then_inc(sem[sk], n)
        return f

    blk.tensor(run("t"))
    blk.vector(run("v"))
    blk.scalar(run("s"))
    blk.sync(run("sy"))
    blk.gpsimd(run("g"))


def _host_consts():
    p = np.arange(128)[:, None]
    c = np.arange(128)[None, :]
    diag = np.minimum(np.abs(c - p), 16).astype(np.float32)
    corner = np.minimum(np.abs(128 + c - p), 16).astype(np.float32)
    dist = np.concatenate([diag, corner], axis=1)
    zm = (c >= p).astype(ml_dtypes.bfloat16)
    return dist, zm


def kernel(L0_attn_x, L0_attn_Wq, L0_attn_Wk, L0_attn_Wv, L0_attn_Wo, L0_attn_bo,
           L0_attn_rel_emb, L0_ffn_ln_gamma, L0_ffn_ln_beta, L0_ffn_fc1_q,
           L0_ffn_fc1_scales, L0_ffn_fc1_bias, L0_ffn_fc2_q, L0_ffn_fc2_scales,
           L0_ffn_fc2_bias):
    nc = bass.Bass("TRN2", target_bir_lowering=False)
    build(nc)
    dist, zm = _host_consts()
    asC = np.ascontiguousarray
    base = {
        "wq": asC(np.asarray(L0_attn_Wq, np.float32)),
        "wk": asC(np.asarray(L0_attn_Wk, np.float32)),
        "wv": asC(np.asarray(L0_attn_Wv, np.float32)),
        "wo": asC(np.asarray(L0_attn_Wo, np.float32)),
        "bo": asC(np.asarray(L0_attn_bo, np.float32)),
        "rel": asC(np.asarray(L0_attn_rel_emb, np.float32)),
        "gam": asC(np.asarray(L0_ffn_ln_gamma, np.float32)),
        "bet": asC(np.asarray(L0_ffn_ln_beta, np.float32)),
        "q1": asC(np.asarray(L0_ffn_fc1_q, np.int32)),
        "sc1": asC(np.asarray(L0_ffn_fc1_scales, np.float32)),
        "b1c": asC(np.asarray(L0_ffn_fc1_bias, np.float32).reshape(32, 128).T),
        "q2": asC(np.asarray(L0_ffn_fc2_q, np.int32)),
        "sc2": asC(np.asarray(L0_ffn_fc2_scales, np.float32)),
        "b2c": asC(np.asarray(L0_ffn_fc2_bias, np.float32).reshape(8, 128).T),
        "dist": dist,
        "zm": zm,
    }
    xs = np.asarray(L0_attn_x, np.float32)
    in_maps = [dict(base, x=asC(xs[cix])) for cix in range(8)]
    res = run_bass_kernel_spmd(nc, in_maps, core_ids=list(range(8)))
    out = np.stack([np.asarray(r["y"], np.float32) for r in res.results], axis=0)
    return out


# version 17
# speedup vs baseline: 1.1000x; 1.0162x over previous
import numpy as np
import ml_dtypes
import concourse.bass as bass
import concourse.mybir as mybir
from concourse.bass_utils import run_bass_kernel_spmd
from contextlib import ExitStack

F32 = mybir.dt.float32
BF16 = mybir.dt.bfloat16
I32 = mybir.dt.int32
U8 = mybir.dt.uint8
AF = mybir.ActivationFunctionType
OP = mybir.AluOpType
AX = mybir.AxisListType

S, D, H, HD, FF = 4096, 1024, 16, 64, 4096
NT, NC_, KD, NM = 32, 8, 8, 32
EPS = 1e-5
KB = 1024
ENG2SEM = {"t": "t", "v": "v", "s": "s"}


DMA_STREAMS = ("ld", "lds", "bl0", "bl1", "trA0", "trA1", "xnh0", "xnh1", "qkv0", "qkv1", "w",
               "ctxs", "ctxt", "ssp0", "ssp1", "dq", "trh", "trx", "trg", "gts0",
               "gts1", "y0", "y1")


class EM:

    def __init__(self):
        self.P = {k: [] for k in ("t", "v", "s", "sy", "g")}
        self.C = {k: 0 for k in ("t", "v", "s") + DMA_STREAMS}
        self.lastwait = {}

    def op(self, ek, fn, inc=False, sem=None):
        if inc or sem:
            sk = sem or ENG2SEM[ek]
            n = 16 if sk in DMA_STREAMS else 1
            self.C[sk] += n
            self.P[ek].append((fn, sk, n))
            return self.C[sk]
        self.P[ek].append((fn, None, 0))
        return None

    def wait(self, ek, sk, val=None):
        if sk in DMA_STREAMS:
            val = self.C[sk]
        if not val or val <= 0:
            return
        key = (ek, sk)
        if self.lastwait.get(key, -1) >= val:
            return
        self.lastwait[key] = val
        self.P[ek].append((("W", sk, val), None, 0))


def build(nc: bass.Bass, phases=("A", "B", "C"), dbg_spill=False, bopt=None):
    bopt = dict({"nqb": NC_, "nh": H, "do_av": True, "do_wo": True, "wo_stage": 4,
                 "s_nospill": False, "s_copy": False, "par": 0}, **(bopt or {}))
    ap = lambda name, shape, dt: nc.dram_tensor(name, shape, dt, kind="ExternalInput").ap()
    X = ap("x", [S, D], F32)
    Wq, Wk, Wv, Wo = (ap(n, [D, D], F32) for n in ("wq", "wk", "wv", "wo"))
    BO = ap("bo", [D], F32)
    REL = ap("rel", [17], F32)
    GAM, BET = ap("gam", [D], F32), ap("bet", [D], F32)
    Q1 = ap("q1", [D, FF], I32)
    SC1 = ap("sc1", [8, FF], F32)
    B1C = ap("b1c", [128, 32], F32)
    Q2 = ap("q2", [FF, D], I32)
    SC2 = ap("sc2", [32, D], F32)
    B2C = ap("b2c", [128, 8], F32)
    DIST = ap("dist", [128, 256], F32)
    ZM = ap("zm", [128, 128], BF16)

    Y = nc.dram_tensor("y", [S, D], F32, kind="ExternalOutput").ap()
    kscr = "ExternalOutput" if dbg_spill else "Internal"
    scr = lambda name, shape, dt: nc.dram_tensor(name, shape, dt, kind=kscr).ap()
    XN16D = scr("xn16d", [S, D], BF16)
    H16D = scr("h16d", [S, D], BF16)
    QTD = scr("qtd", [D, S], BF16)
    KTD = scr("ktd", [D, S], BF16)
    VAD = scr("vad", [S, H * 65], BF16)
    CTXD = scr("ctxd", [S, D], BF16)
    SD = scr("sd", [S, D], BF16)
    GTD = scr("gtd", [D, S], BF16)

    e = EM()

    def drain_guard(ek, n=2):
        mk = None
        for _ in range(n):
            if ek == "v":
                mk = e.op("v", lambda g: g.tensor_copy(junkc[:, 8:16], junkc[:, 0:8]),
                          inc=True)
            else:
                mk = e.op("s", lambda g: g.activation(out=junkc[:, 8:16],
                                                      in_=junkc[:, 0:8],
                                                      func=AF.Copy), inc=True)
        e.wait(ek, ENG2SEM[ek], mk)

    def parwait(ek):
        for sk in ("t", "v", "s") + DMA_STREAMS:
            if ENG2SEM.get(ek) != sk:
                e.wait(ek, sk, e.C[sk])
    with ExitStack() as es:
        E_ = es.enter_context
        AR = E_(nc.sbuf_tensor("arena", [128, 204 * KB], U8)).ap()

        def carve(off_kb, shape, dt):
            nb = int(np.prod(shape[1:])) * {F32: 4, BF16: 2, I32: 4}[dt]
            off = int(off_kb * KB)
            assert off % 64 == 0 and off + nb <= 204 * KB, (off_kb, shape)
            v = AR[:, off:off + nb].bitcast(dt)
            if len(shape) > 2:
                dims = {f"d{i}": shape[i] for i in range(1, len(shape))}
                pat = ("p (" + " ".join(dims) + ") -> p " + " ".join(dims))
                v = v.rearrange(pat, **{k: dims[k] for k in list(dims)[:-1]})
            return v

        PS = E_(nc.psum_tensor("ps", [128, 4096], F32)).ap() if False else \
            E_(nc.psum_tensor("ps", [128, 4096], F32))
        P012 = PS[:, 0:1536]
        P345 = PS[:, 1536:3072]
        P6 = PS[:, 3072:3584]
        P7 = PS[:, 3584:4096]
        sem = {k: E_(nc.semaphore(f"sem_{k}")) for k in ("t", "v", "s") + DMA_STREAMS}
        blk = E_(nc.Block())

        xnT = carve(0, [128, KD, S], BF16)
        wq_b = carve(64, [128, KD, D], BF16)
        wk_b = carve(80, [128, KD, D], BF16)
        wv_b = carve(96, [128, KD, D], BF16)
        x_ring = carve(112, [128, 4, D], F32)
        sq_scr = carve(128, [128, D], F32)
        ht_scr = carve(132, [128, D], F32)
        xn_st = carve(136, [128, 2, D], BF16)
        h_st = carve(140, [128, 2, D], BF16)
        q_st = carve(144, [128, 2, 512], BF16)
        k_st = carve(146, [128, 2, 512], BF16)
        v_st = carve(148, [128, 2, H, 65], BF16)
        gam_b = carve(153, [128, D], F32)
        bet_b = carve(157, [128, D], F32)
        colsA = carve(161, [128, 64], F32)
        bo_b = carve(187, [128, D], F32)
        B_dc = carve(191, [128, 256], F32)
        zm_sb = carve(192, [128, 128], BF16)
        rel_bc = carve(192.25, [128, 17], F32)
        r16c = carve(192.375, [128, 1], F32)
        ri16c = carve(192.4375, [128, 1], F32)
        a16c = carve(192.5, [128, 1], F32)
        epsc = carve(192.5625, [128, 1], F32)
        tmpB = carve(193, [128, 256], F32)
        tmpB2 = carve(195, [128, 256], F32)
        rlc = carve(196, [128, 8], F32)
        kT = carve(0, [128, KD, S], BF16)
        v_sb = carve(64, [128, NT, H * 65], BF16)
        qT = carve(129, [128, 2, KD, 512], BF16)
        E_sb = carve(145, [128, 2, 2048], BF16)
        wo_b = carve(153, [128, KD, D], BF16)
        ctx_sb = carve(169, [128, 4, D], BF16)
        ctxT = carve(177, [128, KD, 512], BF16)
        s_st = carve(185, [128, 2, 512], BF16)
        w1_b = carve(0, [128, 8, FF], BF16)
        w2_b = carve(64, [128, 32, D], BF16)
        yT = carve(128, [128, NM, 512], BF16)
        q1q = carve(129, [128, 2, 1024], I32)
        sc1row_q = carve(137, [128, 2, 1024], F32)
        ones1 = carve(194, [128, 128], F32)
        hT_c = carve(160, [128, KD, 512], BF16)
        xnT_c = carve(168, [128, KD, 512], BF16)
        x_c = carve(176, [128, 2, D], F32)
        s_c = carve(184, [128, 2, D], BF16)
        gt_st = carve(188, [128, 2, 512], BF16)
        g_st = carve(190, [128, 2, D], BF16)
        y_st = carve(194, [128, 2, D], F32)
        b1col = carve(202, [128, 32], F32)
        junkc = carve(203, [128, 16], F32)
        hb2col = carve(202.25, [128, 8], F32)

        def dma(ek, st, out, in_, **kw):
            return e.op(ek, lambda g, out=out, in_=in_, kw=kw: g.dma_start(out=out, in_=in_, **kw), sem=st)

        def vop(fn, inc=False):
            return e.op("v", fn, inc=inc)

        def sop(fn, inc=False):
            return e.op("s", fn, inc=inc)

        m_wcast = None
        for (w_sb, Wsrc) in ((wq_b, Wq), (wk_b, Wk), (wv_b, Wv)):
            Wr = Wsrc.rearrange("(k p) n -> k p n", p=128)
            for k in range(KD):
                m_wcast = dma("g", "w", w_sb[:, k, :], Wr[k])
        m_gb = dma("sy", "ld", gam_b[:], GAM.unsqueeze(0).to_broadcast((128, D)))
        m_bb = dma("sy", "ld", bet_b[:], BET.unsqueeze(0).to_broadcast((128, D)))
        dma("sy", "ld", bo_b[:], BO.unsqueeze(0).to_broadcast((128, D)))
        dma("sy", "ld", rel_bc[:], REL.unsqueeze(0).to_broadcast((128, 17)))
        m_r16 = dma("sy", "ld", r16c[:], REL[16:17].unsqueeze(0).to_broadcast((128, 1)))
        m_dist = dma("sy", "ld", tmpB2[:], DIST)
        m_zm = dma("sy", "ld", zm_sb[:], ZM)
        m_b1c = dma("sy", "ld", b1col[:], B1C)
        m_b2craw = dma("sy", "ld", hb2col[:], B2C)

        e.wait("v", "ld")
        vop(lambda g: g.memset(epsc[:], EPS), inc=True)
        vop(lambda g: g.memset(v_st[:, 0, :, 64:65], 1.0), inc=True)
        vop(lambda g: g.memset(v_st[:, 1, :, 64:65], 1.0), inc=True)
        vop(lambda g: g.memset(ones1[0:1, :], 1.0), inc=True)
        vop(lambda g: g.reciprocal(ri16c[:], r16c[:]), inc=True)
        mk0 = vop(lambda g: g.tensor_scalar(out=a16c[:], in0=r16c[:], scalar1=0.125,
                                            scalar2=None, op0=OP.mult), inc=True)
        e.wait("v", "v", mk0)
        mk0 = vop(lambda g: g.tensor_scalar(out=rel_bc[:], in0=rel_bc[:],
                                            scalar1=ri16c[:], scalar2=None,
                                            op0=OP.mult), inc=True)
        e.wait("v", "v", mk0)
        e.wait("v", "ld")
        for r in range(17):
            dst = B_dc if r == 0 else tmpB
            mk0 = vop(lambda g, r=r, dst=dst: g.tensor_scalar(
                out=dst[:], in0=tmpB2[:], scalar1=float(r), scalar2=rel_bc[:, r:r + 1],
                op0=OP.is_equal, op1=OP.mult), inc=True)
            e.wait("v", "v", mk0)
            if r > 0:
                mk0 = vop(lambda g: g.tensor_tensor(out=B_dc[:], in0=B_dc[:],
                                                    in1=tmpB[:], op=OP.add), inc=True)
                e.wait("v", "v", mk0)
        e.wait("v", "ld")
        m_setup = vop(lambda g: g.tensor_scalar(out=hb2col[:], in0=hb2col[:],
                                                scalar1=0.5, scalar2=None, op0=OP.mult),
                      inc=True)
        v_setup_end = m_setup
        s_prev_chunk = [0]

        Xr = X.rearrange("(i p) d -> i p d", p=128)
        XN16r = XN16D.rearrange("(i p) d -> i p d", p=128)
        H16r = H16D.rearrange("(i p) d -> i p d", p=128)
        QTr = QTD.rearrange("(m p) q -> m p q", p=128)
        KTr = KTD.rearrange("(m p) q -> m p q", p=128)
        VAr = VAD.rearrange("(i p) c -> i p c", p=128)

        m_xload = [0] * NT
        m_a1p1 = [0] * NT
        m_sq = [0] * NT
        m_hv = [0] * NT
        m_xnv = [0] * NT
        m_xnsp = [0] * NT
        m_hsp = [0] * NT
        m_tr = [0] * NC_
        qk_ev = {}
        qk_sp = {}
        v_ev = {}
        v_sp = {}
        last_bank_ev = {}
        C_ = lambda i, j: colsA[:, (i % 4) * 16 + j:(i % 4) * 16 + j + 1]

        def emit_trA(n):
            e.wait("sy", "xnh%d" % (n % 2))
            for k in range(KD):
                m_tr[n] = dma("sy", "trA%d" % (n % 2), xnT[:, k, 512 * n:512 * (n + 1)],
                              XN16D[512 * n:512 * (n + 1), 128 * k:128 * (k + 1)],
                              transpose=True)

        def emit_A1(n):
            tiles = list(range(4 * n, 4 * n + 4))
            for i in tiles:
                if i >= 4:
                    e.wait("sy", "v", m_hv[i - 4])
                m_xload[i] = dma("sy", "ld", x_ring[:, i % 4, :], Xr[i])
            if n >= 1:
                emit_trA(n - 1)
            e.wait("v", "ld")
            e.wait("v", "v", v_setup_end)
            if n > 0:
                e.wait("v", "s", s_prev_chunk[0])
            for i in tiles:
                xt = x_ring[:, i % 4, :]
                e.wait("v", "ld")
                vop(lambda g, xt=xt: g.tensor_tensor(out=sq_scr[:], in0=xt, in1=xt,
                                                     op=OP.mult))
                vop(lambda g, i=i: g.tensor_reduce(out=C_(i, 0), in_=sq_scr[:],
                                                   axis=AX.X, op=OP.add), inc=True)
                mk = vop(lambda g, xt=xt, i=i: g.tensor_reduce(
                    out=C_(i, 1), in_=xt, axis=AX.X, op=OP.add), inc=True)
            e.wait("v", "v", mk)
            for i in tiles:
                vop(lambda g, i=i: g.tensor_scalar(out=C_(i, 2), in0=C_(i, 1),
                                                   scalar1=1.0 / D, scalar2=None,
                                                   op0=OP.mult), inc=True)
                mk = vop(lambda g, i=i: g.tensor_scalar(
                    out=C_(i, 4), in0=C_(i, 0), scalar1=1.0 / D, scalar2=epsc[:],
                    op0=OP.mult, op1=OP.add), inc=True)
            e.wait("v", "v", mk)
            for i in tiles:
                mk = vop(lambda g, i=i: g.tensor_tensor(out=C_(i, 3), in0=C_(i, 2),
                                                        in1=C_(i, 2), op=OP.mult),
                         inc=True)
            e.wait("v", "v", mk)
            for i in tiles:
                mk = vop(lambda g, i=i: g.tensor_tensor(out=C_(i, 5), in0=C_(i, 4),
                                                        in1=C_(i, 3), op=OP.subtract),
                         inc=True)
            e.wait("s", "v", mk)
            for i in tiles:
                mks = sop(lambda g, i=i: g.activation(out=C_(i, 7), in_=C_(i, 5),
                                                      func=AF.Sqrt), inc=True)
            e.wait("v", "s", mks)
            for i in tiles:
                vop(lambda g, i=i: g.reciprocal(C_(i, 8), C_(i, 7)), inc=True)
                mk = vop(lambda g, i=i: g.reciprocal(C_(i, 6), C_(i, 5)), inc=True)
            e.wait("s", "v", mk)
            for i in tiles:
                mks = sop(lambda g, i=i: g.activation(out=C_(i, 9), in_=C_(i, 6),
                                                      func=AF.Copy, scale=-EPS,
                                                      bias=1.0 + EPS), inc=True)
            e.wait("s", "s", mks)
            for i in tiles:
                mks = sop(lambda g, i=i: g.activation(out=C_(i, 10), in_=C_(i, 9),
                                                      func=AF.Sqrt), inc=True)
            s_prev_chunk[0] = mks
            e.wait("v", "s", mks)
            for i in tiles:
                mk = vop(lambda g, i=i: g.reciprocal(C_(i, 11), C_(i, 10)), inc=True)
            e.wait("v", "v", mk)
            for i in tiles:
                mk = vop(lambda g, i=i: g.tensor_tensor(out=C_(i, 13), in0=C_(i, 8),
                                                        in1=C_(i, 11), op=OP.mult),
                         inc=True)
            e.wait("v", "v", mk)
            for i in tiles:
                vop(lambda g, i=i: g.tensor_tensor(out=C_(i, 12), in0=C_(i, 2),
                                                   in1=C_(i, 8), op=OP.mult), inc=True)
                mk = vop(lambda g, i=i: g.tensor_tensor(out=C_(i, 14), in0=C_(i, 2),
                                                        in1=C_(i, 13), op=OP.mult),
                         inc=True)
            e.wait("v", "v", mk)
            for i in tiles:
                xt = x_ring[:, i % 4, :]
                sl = i % 2
                if i >= 2:
                    e.wait("v", "xnh%d" % (((i - 2) // 4) % 2))
                m_xnv[i] = vop(lambda g, xt=xt, i=i, sl=sl: g.tensor_scalar(
                    out=xn_st[:, sl, :], in0=xt, scalar1=C_(i, 8), scalar2=C_(i, 12),
                    op0=OP.mult, op1=OP.subtract), inc=True)
                vop(lambda g, xt=xt, i=i: g.tensor_scalar(
                    out=ht_scr[:], in0=xt, scalar1=C_(i, 13), scalar2=C_(i, 14),
                    op0=OP.mult, op1=OP.subtract))
                vop(lambda g: g.tensor_tensor(out=ht_scr[:], in0=ht_scr[:],
                                              in1=gam_b[:], op=OP.mult))
                m_hv[i] = vop(lambda g, sl=sl: g.tensor_tensor(
                    out=h_st[:, sl, :], in0=ht_scr[:], in1=bet_b[:], op=OP.add),
                    inc=True)
                e.wait("g", "v", m_xnv[i])
                m_xnsp[i] = dma("g", "xnh%d" % ((i // 4) % 2), XN16r[i], xn_st[:, sl, :])
                e.wait("g", "v", m_hv[i])
                m_hsp[i] = dma("g", "xnh%d" % ((i // 4) % 2), H16r[i], h_st[:, sl, :])

        def emit_A2(n):
            tiles = list(range(4 * n, 4 * n + 4))
            e.wait("t", "trA%d" % (n % 2))
            if n == 0:
                e.wait("t", "w")
                e.wait("t", "v", m_setup)
                e.wait("s", "v", m_setup)
            nwin = slice(512 * n, 512 * (n + 1))
            for (nm, w_sb, stg, dstD) in (("q", wq_b, q_st, QTr), ("k", wk_b, k_st, KTr)):
                for m in range(KD):
                    gidx = n * KD + m
                    ps = (P6 if m % 2 == 0 else P7)[:, 0:512]
                    e.wait("t", "s", last_bank_ev.get(m % 2, 0))
                    for k in range(KD):
                        e.op("t", lambda g, ps=ps, w=w_sb[:, k, 128 * m:128 * (m + 1)],
                             r=xnT[:, k, nwin], k=k: g.matmul(
                                 ps, w, r, start=(k == 0), stop=(k == KD - 1),
                                 skip_group_check=True),
                             inc=(k == KD - 1))
                    tmark = e.C["t"]
                    sl = m % 2
                    e.wait("s", "t", tmark)
                    e.wait("s", "qkv%d" % (gidx % 2))
                    if nm == "q":
                        mev = sop(lambda g, ps=ps, sl=sl: g.activation(
                            out=q_st[:, sl, :], in_=ps, func=AF.Copy), inc=True)
                        qk_sp[(nm, gidx)] = dma("s", "qkv%d" % (gidx % 2), dstD[m][:, nwin], q_st[:, sl, :])
                    else:
                        mev = sop(lambda g, ps=ps, sl=sl: g.activation(
                            out=k_st[:, sl, :], in_=ps, func=AF.Copy,
                            scale=a16c[:]), inc=True)
                        qk_sp[(nm, gidx)] = dma("s", "qkv%d" % (gidx % 2), dstD[m][:, nwin], k_st[:, sl, :])
                    qk_ev[(nm, gidx)] = mev
                    last_bank_ev[m % 2] = mev
            for i in tiles:
                pv = (P012 if i % 2 == 0 else P345)[:, 0:1024]
                e.wait("t", "s", v_ev.get(i - 2, 0))
                for k in range(KD):
                    for hh in range(2):
                        e.op("t", lambda g, pv=pv, hh=hh,
                             l=xnT[:, k, 128 * i:128 * (i + 1)],
                             r=wv_b[:, k, 512 * hh:512 * (hh + 1)], k=k: g.matmul(
                                 pv[:, 512 * hh:512 * (hh + 1)], l, r,
                                 start=(k == 0), stop=(k == KD - 1),
                                 skip_group_check=True),
                             inc=(k == KD - 1 and hh == 1))
                tmark = e.C["t"]
                sl = i % 2
                e.wait("s", "t", tmark)
                e.wait("s", "qkv%d" % (i % 2))
                v_ev[i] = sop(lambda g, pv=pv, sl=sl: g.activation(
                    out=v_st[:, sl, :, 0:64],
                    in_=pv.rearrange("p (h e) -> p h e", h=H), func=AF.Copy),
                    inc=True)
                e.wait("s", "v", m_setup)
                v_sp[i] = dma("s", "qkv%d" % (i % 2), VAr[i], v_st[:, sl].rearrange("p h e -> p (h e)"))

        for n in range(NC_ + 2):
            if n == NC_:
                emit_trA(NC_ - 1)
            if n >= 2:
                emit_A2(n - 2)
            if n < NC_:
                emit_A1(n)

        A_end_t = e.C["t"]
        A_end_v = e.C["v"]

        if "B" not in phases:
            _emit_block(nc, e, blk, sem, None)
            return nc

        e.wait("g", "v", A_end_v)
        wo_cast = None
        WoR = Wo.rearrange("(k p) n -> k p n", p=128)
        for k in range(KD):
            wo_cast = dma("g", "w", wo_b[:, k, :], WoR[k])

        e.wait("t", "v", A_end_v)
        e.wait("v", "ld")
        KTD_r = KTD.rearrange("(t p) q -> p t q", p=128)
        QTD_r = QTD.rearrange("(t p) q -> p t q", p=128)
        VAD_r = VAD.rearrange("(t p) c -> p t c", p=128)
        CTXr = CTXD.rearrange("(i p) d -> i p d", p=128)
        SDr = SD.rearrange("(i p) d -> i p d", p=128)

        exp_half = {}
        av_half = {}
        B_marks = {"scores": 0, "av": 0}
        last_scores_qb = [0] * NC_
        ctx_ev = {}
        ctx_sp = [0] * NC_
        ctxT_tr = [0] * NC_
        wo_end = [0] * NC_
        s_ev = {}
        s_sp = {}
        ci_glob = [0]

        def emit_wo_group(qbw, gidx):
            qs, half = gidx // 2, gidx % 2
            pso = (P6 if gidx % 2 == 0 else P7)[:, 0:512]
            prev_sev = s_ev.get((qbw, gidx - 2), s_ev.get((qbw - 1, gidx + 6), 0))
            e.wait("t", "v", prev_sev)
            e.wait("t", "ctxt")
            if qbw == 0 and gidx == 0:
                e.wait("t", "w")
            for k in range(KD):
                e.op("t", lambda g, pso=pso,
                     l=ctxT[:, k, 128 * qs:128 * (qs + 1)],
                     r=wo_b[:, k, 512 * half:512 * (half + 1)], k=k: g.matmul(
                         pso, l, r, start=(k == 0),
                         stop=(k == KD - 1), skip_group_check=True),
                     inc=(k == KD - 1))
            t_o = e.C["t"]
            sl = gidx % 2
            e.wait("v", "t", t_o)
            e.wait("v", "ssp%d" % (gidx % 2))
            s_ev[(qbw, gidx)] = vop(lambda g, sl=sl, half=half, pso=pso:
                                    g.tensor_tensor(
                                        out=s_st[:, sl, :], in0=pso,
                                        in1=bo_b[:, 512 * half:512 * (half + 1)],
                                        op=OP.add), inc=True)
            e.wait("g", "v", s_ev[(qbw, gidx)])
            s_sp[(qbw, gidx)] = dma(
                "g", "ssp%d" % (gidx % 2),
                SDr[4 * qbw + qs][:, 512 * half:512 * (half + 1)],
                s_st[:, sl, :])
            if gidx == 7:
                wo_end[qbw] = e.C["t"]

        def emit_b_loads(qb):
            e.wait("sy", "t", A_end_t)
            e.wait("sy", "xnh0"); e.wait("sy", "xnh1")
            e.wait("sy", "qkv0"); e.wait("sy", "qkv1")
            st = "bl%d" % (qb % 2)
            dma("sy", st, kT[:, :, 512 * qb:512 * (qb + 1)],
                KTD_r[:, :, 512 * qb:512 * (qb + 1)])
            dma("sy", st, v_sb[:, 4 * qb:4 * qb + 4, :],
                VAD_r[:, 4 * qb:4 * qb + 4, :])
            if qb >= 2:
                e.wait("sy", "t", last_scores_qb[qb - 2])
            dma("sy", st, qT[:, qb % 2], QTD_r[:, :, 512 * qb:512 * (qb + 1)])

        emit_b_loads(0)
        for qb in range(bopt["nqb"]):
            J = 4 * qb + 4

            jl = list(range(J))

            def emit_scores_half(u, j0, hp, jj):
                kt_tile = hp
                ebuf = E_sb[:, u % 2]
                j = j0 + jj
                sd_ = j - 4 * qb
                if jj == 0:
                    e.wait("t", "bl%d" % (qb % 2))
                e.wait("t", "s", exp_half.get((u - 1, jj), 0))
                for par in range(2):
                    poff = 64 * par
                    def smm(g, j=j, sd_=sd_, jj=jj, par=par, kt_tile=kt_tile,
                            poff=poff, qb2=qb % 2):
                        l_ap = kT[poff:poff + 64, kt_tile, 128 * j:128 * (j + 1)]
                        lo = 1024 * jj + 512 * par
                        if sd_ < 0:
                            r_ap = qT[poff:poff + 64, qb2, kt_tile, 0:512]
                            o_ap = PS[:, lo:lo + 512]
                        else:
                            r_ap = qT[poff:poff + 64, qb2, kt_tile, 128 * sd_:512]
                            o_ap = PS[:, lo + 128 * sd_:lo + 512]
                        return g.matmul(o_ap, l_ap, r_ap, start=True, stop=True,
                                        skip_group_check=True)
                    e.op("t", smm, inc=(par == 1))
                t_sc = e.C["t"]
                B_marks["scores"] = t_sc
                v_band = None
                breg = None
                if sd_ == -1:
                    breg = (0, 128, 128)
                elif 0 <= sd_ <= 3:
                    breg = (128 * sd_, min(256, 512 - 128 * sd_), 0)
                if breg is not None:
                    o0, w, boff = breg
                    e.wait("v", "t", t_sc)
                    for par in range(2):
                        lo = 1024 * jj + 512 * par + o0
                        v_band = vop(lambda g, lo=lo, w=w, boff=boff:
                                     g.tensor_tensor(out=PS[:, lo:lo + w],
                                                     in0=PS[:, lo:lo + w],
                                                     in1=B_dc[:, boff:boff + w],
                                                     op=OP.mult),
                                     inc=True)
                e.wait("s", "t", t_sc)
                if v_band:
                    e.wait("s", "v", v_band)
                e.wait("s", "t", av_half.get((u - 2, jj), 0))
                if 1 <= sd_ <= 3:
                    w = 512 - 128 * sd_
                    for par in range(2):
                        lo2 = 1024 * jj + 512 * par + 128 * sd_
                        exp_half[(u, jj)] = sop(lambda g, ebuf=ebuf, lo2=lo2, w=w:
                                                g.activation(
                                                    out=ebuf[0:128, lo2:lo2 + w],
                                                    in_=PS[:, lo2:lo2 + w],
                                                    func=AF.Exp), inc=True)
                else:
                    exp_half[(u, jj)] = sop(lambda g, ebuf=ebuf, jj=jj:
                                            g.activation(
                                                out=ebuf[0:128, 1024 * jj:1024 * jj + 1024],
                                                in_=PS[:, 1024 * jj:1024 * jj + 1024],
                                                func=AF.Exp), inc=True)
                v_z = None
                if 0 <= sd_ <= 3:
                    e.wait("v", "s", exp_half[(u, jj)])
                    for par in range(2):
                        col = 1024 * jj + 512 * par + 128 * sd_
                        v_z = vop(lambda g, ebuf=ebuf, col=col: g.tensor_tensor(
                            out=ebuf[0:128, col:col + 128],
                            in0=ebuf[0:128, col:col + 128], in1=zm_sb[:],
                            op=OP.mult), inc=True)
                zv_mark[(u, jj)] = v_z

            def emit_av_half(u, j0, hp, jj, first):
                ebuf = E_sb[:, u % 2]
                e.wait("t", "s", exp_half[(u, jj)])
                if zv_mark.get((u, jj)):
                    e.wait("t", "v", zv_mark[(u, jj)])
                if first:
                    prev = ctx_ev.get((qb, hp - 1), 0) if hp > 0 else \
                        (ctx_ev.get((qb - 1, H // 2 - 1), 0) if qb > 0 else 0)
                    e.wait("t", "v", prev)
                j = j0 + jj
                sd_ = j - 4 * qb
                for par in range(2):
                    h = 2 * hp + par
                    cbase = 2048 + 512 * par
                    for qs in range(max(0, sd_), 4):
                        def av(g, jj=jj, j=j, qs=qs, par=par, h=h, cbase=cbase,
                               ebuf=ebuf, J=J):
                            return g.matmul(
                                PS[:, cbase + 65 * qs:cbase + 65 * qs + 65],
                                ebuf[0:128, 1024 * jj + 512 * par + 128 * qs:
                                     1024 * jj + 512 * par + 128 * qs + 128],
                                v_sb[:, j, 65 * h:65 * (h + 1)],
                                start=(j == 0 and qs == 0),
                                stop=(j == J - 1 and qs == 3),
                                skip_group_check=True)
                        e.op("t", av, inc=(par == 1 and qs == 3))
                av_half[(u, jj)] = e.C["t"]
                B_marks["av"] = av_half[(u, jj)]

            zv_mark = {}
            for hp in range(H // 2):
                if hp == 1 and qb + 1 < bopt["nqb"]:
                    emit_b_loads(qb + 1)
                pend = None
                first = True
                for idx, j0 in enumerate(range(0, J, 2)):
                    u = ci_glob[0]
                    ci_glob[0] += 1
                    emit_scores_half(u, j0, hp, 0)
                    if pend is not None:
                        emit_av_half(pend[0], pend[1], hp, 0, first)
                    emit_scores_half(u, j0, hp, 1)
                    if pend is not None:
                        emit_av_half(pend[0], pend[1], hp, 1, False)
                        first = False
                    pend = (u, j0)
                    if idx == 0 and hp >= 1 and qb > 0 and bopt["do_wo"] \
                            and bopt["nh"] == H:
                        emit_wo_group(qb - 1, hp)
                emit_av_half(pend[0], pend[1], hp, 0, first)
                emit_av_half(pend[0], pend[1], hp, 1, False)
                if hp == 0 and qb > 0 and bopt["do_wo"] and bopt["nh"] == H:
                    emit_wo_group(qb - 1, hp)
                t_av = av_half[(pend[0], 1)]
                e.wait("v", "t", t_av)
                if hp == 0 and qb > 0:
                    e.wait("v", "ctxs")
                mkr = None
                for par in range(2):
                    cbase = 2048 + 512 * par
                    for qs in range(4):
                        mkr = vop(lambda g, qs=qs, par=par, cbase=cbase: g.reciprocal(
                            rlc[:, 4 * par + qs:4 * par + qs + 1],
                            PS[:, cbase + 65 * qs + 64:cbase + 65 * qs + 65]),
                            inc=True)
                e.wait("v", "v", mkr)
                for par in range(2):
                    h = 2 * hp + par
                    cbase = 2048 + 512 * par
                    for qs in range(4):
                        ctx_ev[(qb, hp)] = vop(lambda g, qs=qs, h=h, par=par,
                                               cbase=cbase: g.tensor_scalar(
                            out=ctx_sb[:, qs, 64 * h:64 * (h + 1)],
                            in0=PS[:, cbase + 65 * qs:cbase + 65 * qs + 64],
                            scalar1=rlc[:, 4 * par + qs:4 * par + qs + 1],
                            scalar2=None, op0=OP.mult),
                            inc=(par == 1 and qs == 3))
            last_scores_qb[qb] = e.C["t"]
            if not (bopt["do_av"] and bopt["do_wo"] and bopt["nh"] == H):
                continue
            e.wait("g", "v", ctx_ev[(qb, H // 2 - 1)])
            CTXr = CTXD.rearrange("(i p) d -> i p d", p=128)
            for qs in range(4):
                ctx_sp[qb] = dma("g", "ctxs", CTXr[4 * qb + qs], ctx_sb[:, qs, :])
            e.wait("sy", "ctxs")
            if qb > 0:
                e.wait("sy", "t", wo_end[qb - 1])
            for k in range(KD):
                ctxT_tr[qb] = dma("sy", "ctxt", ctxT[:, k, :],
                                  CTXD[512 * qb:512 * (qb + 1), 128 * k:128 * (k + 1)],
                                  transpose=True)
        if bopt["do_av"] and bopt["do_wo"] and bopt["nh"] == H and bopt["nqb"] == NC_:
            for gidx in range(8):
                emit_wo_group(NC_ - 1, gidx)

        B_end_t = e.C["t"]
        B_end_v = e.C["v"]
        B_end_s = e.C["s"]

        if "C" not in phases:
            _emit_block(nc, e, blk, sem, None)
            return nc

        Q1r = Q1.rearrange("(g p) f -> g p f", p=128)
        Q2r = Q2.rearrange("(g p) d -> g p d", p=128)
        B_scores_final = B_marks["scores"]
        B_last_av = B_marks["av"]
        e.wait("sy", "t", B_scores_final)
        e.wait("t", "s", B_end_s)
        all_stt = []
        bc_idx = 0

        def dq_unit(w_dst, q_src_ap, sc_row_slice, w2=False):
            psb = PS[:, 512 * (bc_idx % 2):512 * (bc_idx % 2) + 512]
            if bc_idx >= 2:
                e.wait("t", "v", all_stt[bc_idx - 2])
            e.op("t", lambda g, psb=psb, r=sc_row_slice: g.matmul(
                psb, ones1[0:1, :], r, start=True, stop=True,
                skip_group_check=True), inc=True)
            tbc = e.C["t"]
            e.wait("v", "t", tbc)
            all_stt.append(vop(lambda g, w_dst=w_dst, q=q_src_ap, psb=psb:
                               g.scalar_tensor_tensor(
                                   out=w_dst, in0=q, scalar=-128.0, in1=psb,
                                   op0=OP.add, op1=OP.mult), inc=True))

        for gix in range(8):
            for qt in range(4):
                it = gix * 4 + qt
                sl = it % 2
                if it >= 2:
                    e.wait("sy", "v", all_stt[2 * (it - 2) + 1])
                mq = dma("sy", "dq", q1q[:, sl, :], Q1r[gix][:, 1024 * qt:1024 * (qt + 1)])
                msc = dma("sy", "dq", sc1row_q[0:1, sl, :],
                          SC1[gix:gix + 1, 1024 * qt:1024 * (qt + 1)])
                e.wait("t", "dq")
                e.wait("v", "dq")
                for c in range(2):
                    dq_unit(
                        w1_b[:, gix, 1024 * qt + 512 * c:1024 * qt + 512 * (c + 1)],
                        q1q[:, sl, 512 * c:512 * (c + 1)],
                        sc1row_q[0:1, sl, 512 * c:512 * (c + 1)])
                    bc_idx += 1
        w1_done = all_stt[-1]
        e.wait("v", "t", B_last_av)
        e.wait("sy", "v", w1_done)
        n_w1 = len(all_stt)
        for g2 in range(32):
            sl = g2 % 2
            if g2 >= 2:
                e.wait("sy", "v", all_stt[n_w1 + 2 * (g2 - 2) + 1])
            mq = dma("sy", "dq", q1q[:, sl, :], Q2r[g2])
            msc = dma("sy", "dq", sc1row_q[0:1, sl, :], SC2[g2:g2 + 1, :])
            e.wait("t", "dq")
            e.wait("v", "dq")
            for c in range(2):
                dq_unit(w2_b[:, g2, 512 * c:512 * (c + 1)],
                        q1q[:, sl, 512 * c:512 * (c + 1)],
                        sc1row_q[0:1, sl, 512 * c:512 * (c + 1)], w2=True)
                bc_idx += 1
        w2_done = all_stt[-1]

        fc1_end = [0] * NC_
        fc2_end = [0] * NC_
        silu_mark = {}
        z_v = {}
        z_s = {}
        gt_sp = {}
        g_tr = [0] * NC_
        comb_v = {}
        y_dma = {}
        hT_ready = [0] * NC_
        xnT_ready = [0] * NC_
        sx_ready = {}

        def emit_hT_loads(n):
            if n == 0:
                e.wait("sy", "v", w1_done)
                e.wait("sy", "ctxs"); e.wait("sy", "ssp0"); e.wait("sy", "ssp1")
                e.wait("s", "ld")
            else:
                e.wait("sy", "t", fc1_end[n - 1])
            for k in range(KD):
                hT_ready[n] = dma("sy", "trh", hT_c[:, k, :],
                                  H16D[512 * n:512 * (n + 1), 128 * k:128 * (k + 1)],
                                  transpose=True)

        def emit_xnT_loads(n):
            if n > 0:
                e.wait("sy", "v", z_v.get((n - 1, 7), 0))
            for k in range(KD):
                xnT_ready[n] = dma("sy", "trx", xnT_c[:, k, :],
                                   XN16D[512 * n:512 * (n + 1), 128 * k:128 * (k + 1)],
                                   transpose=True)
        emit_hT_loads(0)
        emit_xnT_loads(0)
        for n in range(NC_):
            tiles = list(range(4 * n, 4 * n + 4))
            e.wait("t", "trh")
            if n == 0:
                e.wait("t", "v", w1_done)
            for m in range(NM):
                ps = (P6 if m % 2 == 0 else P7)[:, 0:512]
                e.wait("t", "s", silu_mark.get((n, m - 2), 0))
                if n > 0 and m < 2:
                    e.wait("t", "s", silu_mark.get((n - 1, NM - 2 + m), 0))
                hTb = hT_c
                for k in range(KD):
                    e.op("t", lambda g, ps=ps, l=w1_b[:, k, 128 * m:128 * (m + 1)],
                         r=hTb[:, k, :], k=k: g.matmul(
                             ps, l, r, start=(k == 0), stop=(k == KD - 1),
                             skip_group_check=True), inc=(k == KD - 1))
                t_m = e.C["t"]
                e.wait("s", "t", t_m)
                if n > 0 and m == 0:
                    e.wait("s", "t", fc2_end[n - 1])
                silu_mark[(n, m)] = sop(lambda g, ps=ps, m=m: g.activation(
                    out=yT[:, m, :], in_=ps, func=AF.Silu,
                    bias=b1col[:, m:m + 1]), inc=True)
            fc1_end[n] = e.C["t"]
            if n + 1 < NC_:
                emit_hT_loads(n + 1)
            e.wait("t", "s", silu_mark[(n, NM - 1)])
            if n == 0:
                e.wait("t", "v", w2_done)
                e.wait("t", "s", B_end_s)
            for mo in range(KD):
                pz = (P012 if mo % 2 == 0 else P345)[:, 0:512]
                e.wait("t", "s", z_s.get((n, mo - 2), 0))
                if n > 0 and mo < 2:
                    e.wait("t", "s", z_s.get((n - 1, KD - 2 + mo), 0))
                for k2 in range(NM):
                    e.op("t", lambda g, pz=pz, l=w2_b[:, k2, 128 * mo:128 * (mo + 1)],
                         r=yT[:, k2, :], k2=k2: g.matmul(
                             pz, l, r, start=(k2 == 0), stop=(k2 == NM - 1),
                             skip_group_check=True), inc=(k2 == NM - 1))
                t_z = e.C["t"]
                e.wait("v", "t", t_z)
                e.wait("v", "trh")
                e.wait("v", "trx")
                z_v[(n, mo)] = vop(lambda g, pz=pz, mo=mo: g.scalar_tensor_tensor(
                    out=pz, in0=pz, scalar=0.5, in1=xnT_c[:, mo, :],
                    op0=OP.mult, op1=OP.add), inc=True)
                sl = mo % 2
                e.wait("s", "v", z_v[(n, mo)])
                e.wait("s", "gts%d" % (mo % 2))
                z_s[(n, mo)] = sop(lambda g, pz=pz, sl=sl, mo=mo: g.activation(
                    out=gt_st[:, sl, :], in_=pz, func=AF.Silu,
                    bias=hb2col[:, mo:mo + 1]), inc=True)
                e.wait("g", "s", z_s[(n, mo)])
                gt_sp[(n, mo)] = dma("g", "gts%d" % (mo % 2), GTD[128 * mo:128 * (mo + 1),
                                            512 * n:512 * (n + 1)], gt_st[:, sl, :])
            fc2_end[n] = e.C["t"]
            if n + 1 < NC_:
                emit_xnT_loads(n + 1)
            e.wait("sy", "gts0"); e.wait("sy", "gts1")
            if n == 0:
                e.wait("sy", "v", w2_done)
            for ii, i in enumerate(tiles):
                sl = i % 2
                e.wait("sy", "v", comb_v.get(i - 2, 0))
                g_tr[n] = dma("sy", "trg", g_st[:, sl, :],
                              GTD[:, 128 * i:128 * (i + 1)], transpose=True)
                m_s = dma("sy", "ld", s_c[:, sl, :], SDr[i])
                m_x = dma("sy", "ld", x_c[:, sl, :], Xr[i])
                e.wait("v", "trg")
                e.wait("v", "ld")
                e.wait("v", "y%d" % (i % 2))
                vop(lambda g, sl=sl: g.tensor_tensor(
                    out=y_st[:, sl, :], in0=s_c[:, sl, :], in1=g_st[:, sl, :],
                    op=OP.mult))
                comb_v[i] = vop(lambda g, sl=sl: g.tensor_tensor(
                    out=y_st[:, sl, :], in0=y_st[:, sl, :], in1=x_c[:, sl, :],
                    op=OP.add), inc=True)
                e.wait("g", "v", comb_v[i])
                y_dma[i] = dma("g", "y%d" % (i % 2),
                               Y.rearrange("(i p) d -> i p d", p=128)[i],
                               y_st[:, sl, :])

        _emit_block(nc, e, blk, sem, None)
    return nc


def _emit_block(nc, e, blk, sem, final_qg):
    for st in DMA_STREAMS:
        e.wait("g", st, e.C[st])
    progs = e.P

    def run(ek):
        def f(eng):
            for (fn, sk, n) in progs[ek]:
                if isinstance(fn, tuple) and fn[0] == "W":
                    eng.wait_ge(sem[fn[1]], fn[2])
                else:
                    instr = fn(eng)
                    if sk is not None:
                        instr.then_inc(sem[sk], n)
        return f

    blk.tensor(run("t"))
    blk.vector(run("v"))
    blk.scalar(run("s"))
    blk.sync(run("sy"))
    blk.gpsimd(run("g"))


def _host_consts():
    p = np.arange(128)[:, None]
    c = np.arange(128)[None, :]
    diag = np.minimum(np.abs(c - p), 16).astype(np.float32)
    corner = np.minimum(np.abs(128 + c - p), 16).astype(np.float32)
    dist = np.concatenate([diag, corner], axis=1)
    zm = (c >= p).astype(ml_dtypes.bfloat16)
    return dist, zm


def kernel(L0_attn_x, L0_attn_Wq, L0_attn_Wk, L0_attn_Wv, L0_attn_Wo, L0_attn_bo,
           L0_attn_rel_emb, L0_ffn_ln_gamma, L0_ffn_ln_beta, L0_ffn_fc1_q,
           L0_ffn_fc1_scales, L0_ffn_fc1_bias, L0_ffn_fc2_q, L0_ffn_fc2_scales,
           L0_ffn_fc2_bias):
    nc = bass.Bass("TRN2", target_bir_lowering=False)
    build(nc)
    dist, zm = _host_consts()
    asC = np.ascontiguousarray
    base = {
        "wq": asC(np.asarray(L0_attn_Wq, np.float32)),
        "wk": asC(np.asarray(L0_attn_Wk, np.float32)),
        "wv": asC(np.asarray(L0_attn_Wv, np.float32)),
        "wo": asC(np.asarray(L0_attn_Wo, np.float32)),
        "bo": asC(np.asarray(L0_attn_bo, np.float32)),
        "rel": asC(np.asarray(L0_attn_rel_emb, np.float32)),
        "gam": asC(np.asarray(L0_ffn_ln_gamma, np.float32)),
        "bet": asC(np.asarray(L0_ffn_ln_beta, np.float32)),
        "q1": asC(np.asarray(L0_ffn_fc1_q, np.int32)),
        "sc1": asC(np.asarray(L0_ffn_fc1_scales, np.float32)),
        "b1c": asC(np.asarray(L0_ffn_fc1_bias, np.float32).reshape(32, 128).T),
        "q2": asC(np.asarray(L0_ffn_fc2_q, np.int32)),
        "sc2": asC(np.asarray(L0_ffn_fc2_scales, np.float32)),
        "b2c": asC(np.asarray(L0_ffn_fc2_bias, np.float32).reshape(8, 128).T),
        "dist": dist,
        "zm": zm,
    }
    xs = np.asarray(L0_attn_x, np.float32)
    in_maps = [dict(base, x=asC(xs[cix])) for cix in range(8)]
    res = run_bass_kernel_spmd(nc, in_maps, core_ids=list(range(8)))
    out = np.stack([np.asarray(r["y"], np.float32) for r in res.results], axis=0)
    return out
